# version 01 (final)
# baseline (speedup 1.0000x reference)
import numpy as np
from contextlib import ExitStack, contextmanager

import concourse.bass as bass
import concourse.mybir as mybir
from concourse.bass_utils import run_bass_kernel_spmd
from concourse.bass_core_rust import IndirectOffsetOnAxis

F32 = mybir.dt.float32
BF16 = mybir.dt.bfloat16
U32 = mybir.dt.uint32
I32 = mybir.dt.int32
AF = mybir.ActivationFunctionType
ALU = mybir.AluOpType
AX = mybir.AxisListType

B, S, DM = 16, 2048, 1024
NCORES = 8
BL = B // NCORES
T = BL * S
NH, HD = 16, 64
NT = T // 128
NTS = S // 128
NDC = DM // 128
LN_EPS = 1e-5


def _bf(x):
    import ml_dtypes
    return np.ascontiguousarray(np.asarray(x)).astype(ml_dtypes.bfloat16)


class Sem:
    def __init__(self, ctx, name):
        self.h = ctx.es.enter_context(ctx.nc.semaphore(name))
        self.v = 0


class SemView:

    def __init__(self, sem):
        self._s = sem
        self.b = sem.v
        self.h = sem.h

    def inc(self, instr, n=1):
        instr.then_inc(self._s.h, n)
        self._s.v += n
        return instr

    def tgt(self, k):
        return self.b + k

    @property
    def v(self):
        return self._s.v


class Ctx:
    def __init__(self, nc):
        self.nc = nc
        self.es = ExitStack()
        self._sems = {}
        self._consts = {}
        self._n = 0
        self._frame = self.es
        self.pf = [nc.alloc_psum_tensor(f"ppool_f{i}", [128, 512], F32)
                   for i in range(6)]
        self.pb = [nc.alloc_psum_tensor(f"ppool_b{i}", [128, 1024], BF16)
                   for i in range(2)]

    def sem(self, name):
        if name not in self._sems:
            self._sems[name] = Sem(self, name)
        return SemView(self._sems[name])

    def uname(self, base):
        self._n += 1
        return f"{base}{self._n}"

    def const(self, key, np_arr):
        if key not in self._consts:
            self._consts[key] = self.nc.inline_tensor(
                np.ascontiguousarray(np_arr), f"c_{key}").ap()
        return self._consts[key]

    def sb(self, base, shape, dtype):
        return self.nc.alloc_sbuf_tensor(self.uname(base), shape, dtype)

    def sbf(self, base, shape, dtype):
        return self._frame.enter_context(
            self.nc.sbuf_tensor(self.uname(base), shape, dtype))

    @contextmanager
    def frame(self):
        es = ExitStack()
        old = self._frame
        self._frame = es
        try:
            yield
        finally:
            es.close()
            self._frame = old

    def barrier(self):
        self.nc.all_engine_barrier()


SCALAR_CONSTS = [LN_EPS, 1.0 / DM, -1.0, 0.125, -0.5, 1e-6, 0.5, 2.0, -2.0,
                 0.25, 1e-4, 8.0, 1.0 / 64.0, 64.0, -0.125]


def make_consts(ctx):
    nc = ctx.nc
    c = {}
    ns = len(SCALAR_CONSTS)
    sc_np = np.tile(np.asarray(SCALAR_CONSTS, np.float32)[None, :], (128, 1))
    specs = [
        ("scal", [128, ns], F32, sc_np),
        ("ident_bf", [128, 128], BF16, _bf(np.eye(128, dtype=np.float32))),
        ("ones_row_bf", [1, 128], BF16, _bf(np.ones((1, 128), np.float32))),
        ("ones_row_f32", [1, 128], F32, np.ones((1, 128), np.float32)),
        ("ones_col_bf", [128, 1], BF16, _bf(np.ones((128, 1), np.float32))),
        ("ones_col_f32", [128, 1], F32, np.ones((128, 1), np.float32)),
        ("ones_f32_fh", [128, 128], F32, np.ones((128, 128), np.float32)),
        ("ident_f32", [128, 128], F32, np.eye(128, dtype=np.float32)),
    ]
    sld = ctx.sem("cst_ld")
    with nc.Block() as blk:
        @blk.sync
        def _(s):
            for name, shape, dt, arr in specs:
                c[name] = ctx.sb("cst_" + name, shape, dt)
                d = ctx.const(name, arr)
                sld.inc(s.dma_start(out=c[name][:], in_=d[:]), 16)

        @blk.vector
        def _(v):
            v.wait_ge(sld.h, sld.v)
    for j, val in enumerate(SCALAR_CONSTS):
        nc.const_aps.register(F32, float(val), c["scal"].ap()[:, j:j + 1])
    ctx.barrier()
    return c


def _chain(eng, s_ch, instr):
    s_ch.inc(instr)
    eng.wait_ge(s_ch.h, s_ch.v)
    return instr


def _ln_stats_scalar(sc, p, sq, xt, stt, s_sqa, s_sum, s_ch, i):
    s_sqa.inc(sc.activation(sq[:], xt[:], AF.Square,
                            accum_out=stt["st_ssq"][p][:]))
    sc.wait_ge(s_sqa.h, s_sqa.tgt(i + 1))
    sc.wait_ge(s_sum.h, s_sum.tgt(i + 1))
    _chain(sc, s_ch, sc.mul(stt["st_mean"][p][:], stt["st_sum"][p][:], 1.0 / DM))
    _chain(sc, s_ch, sc.mul(stt["st_ssq"][p][:], stt["st_ssq"][p][:], 1.0 / DM))
    _chain(sc, s_ch, sc.mul(stt["st_tmp"][p][:], stt["st_mean"][p][:],
                            stt["st_mean"][p][:]))
    _chain(sc, s_ch, sc.mul(stt["st_tmp"][p][:], stt["st_tmp"][p][:], -1.0))
    _chain(sc, s_ch, sc.add(stt["st_ssq"][p][:], stt["st_ssq"][p][:],
                            stt["st_tmp"][p][:]))
    _chain(sc, s_ch, sc.activation(stt["st_tmp"][p][:], stt["st_ssq"][p][:],
                                   AF.Ln, bias=LN_EPS))
    return sc.activation(stt["st_rstd"][p][:], stt["st_tmp"][p][:], AF.Exp,
                         scale=-0.5)


def _alloc_ln_stats(ctx, tag):
    return dict(
        st_sum=[ctx.sbf(tag + "s1", [128, 1], F32) for _ in range(2)],
        st_ssq=[ctx.sbf(tag + "s2", [128, 1], F32) for _ in range(2)],
        st_mean=[ctx.sbf(tag + "mu", [128, 1], F32) for _ in range(2)],
        st_tmp=[ctx.sbf(tag + "tp", [128, 1], F32) for _ in range(2)],
        st_rstd=[ctx.sbf(tag + "rs", [128, 1], F32) for _ in range(2)],
    )


def emit_ln(ctx, consts, x_hbm, lnx32_hbm=None, lnxbf_hbm=None, tag="ln"):
    nc = ctx.nc
    NB = 3
    with ctx.frame():
        xt = [ctx.sbf(tag + "x", [128, DM], F32) for _ in range(NB)]
        lnf = [ctx.sbf(tag + "f", [128, DM], F32) for _ in range(NB)]
        lnb = [ctx.sbf(tag + "b", [128, DM], BF16) for _ in range(NB)]
        sq = ctx.sbf(tag + "sq", [128, DM], F32)
        stt = _alloc_ln_stats(ctx, tag)

        s_ld = [ctx.sem(f"SEMFAM_ld{b}") for b in range(NB)]
        s_sum = ctx.sem("SEMFAM_sum")
        s_sqa = ctx.sem("SEMFAM_sqa")
        s_rst = ctx.sem("SEMFAM_rst")
        s_nrm = ctx.sem("SEMFAM_nrm")
        s_cst = ctx.sem("SEMFAM_cst")
        s_ch = ctx.sem("SEMFAM_ch")
        s_st32 = [ctx.sem(f"SEMFAM_st32{b}") for b in range(NB)]
        s_stbf = [ctx.sem(f"SEMFAM_stbf{b}") for b in range(NB)]

        with nc.Block() as blk:
            @blk.sync
            def _(s):
                for i in range(NT):
                    b = i % NB
                    if i >= NB:
                        s.wait_ge(s_nrm.h, s_nrm.tgt(i - NB + 1))
                        s.wait_ge(s_sqa.h, s_sqa.tgt(i - NB + 1))
                        if lnx32_hbm is not None:
                            s.wait_ge(s_st32[b].h, s_st32[b].tgt(16 * (i // NB)))
                        if lnxbf_hbm is not None:
                            s.wait_ge(s_stbf[b].h, s_stbf[b].tgt(16 * (i // NB)))
                    s_ld[b].inc(s.dma_start(
                        out=xt[b][:], in_=x_hbm[i * 128:(i + 1) * 128, :]), 16)
                    if lnx32_hbm is not None:
                        s.wait_ge(s_nrm.h, s_nrm.tgt(i + 1))
                        s_st32[b].inc(s.dma_start(
                            out=lnx32_hbm[i * 128:(i + 1) * 128, :],
                            in_=lnf[b][:]), 16)
                    if lnxbf_hbm is not None:
                        s.wait_ge(s_cst.h, s_cst.tgt(i + 1))
                        s_stbf[b].inc(s.dma_start(
                            out=lnxbf_hbm[i * 128:(i + 1) * 128, :],
                            in_=lnb[b][:]), 16)

            @blk.vector
            def _(v):
                for i in range(NT):
                    b, p = i % NB, i % 2
                    v.wait_ge(s_ld[b].h, s_ld[b].tgt(16 * (i // NB + 1)))
                    if i >= 2:
                        v.wait_ge(s_rst.h, s_rst.tgt(i - 1))
                    s_sum.inc(v.reduce_sum(stt["st_sum"][p][:], xt[b][:],
                                           axis=AX.X))
                    v.wait_ge(s_rst.h, s_rst.tgt(i + 1))
                    s_nrm.inc(v.tensor_scalar(lnf[b][:], xt[b][:],
                                              stt["st_mean"][p][:],
                                              stt["st_rstd"][p][:],
                                              ALU.subtract, ALU.mult))

            @blk.scalar
            def _(sc):
                for i in range(NT):
                    b, p = i % NB, i % 2
                    sc.wait_ge(s_ld[b].h, s_ld[b].tgt(16 * (i // NB + 1)))
                    if i >= 2:
                        sc.wait_ge(s_nrm.h, s_nrm.tgt(i - 1))
                    s_rst.inc(_ln_stats_scalar(sc, p, sq, xt[b], stt,
                                               s_sqa, s_sum, s_ch, i))
                    if lnxbf_hbm is not None:
                        sc.wait_ge(s_nrm.h, s_nrm.tgt(i + 1))
                        s_cst.inc(sc.copy(lnb[b][:], lnf[b][:]))

            @blk.gpsimd
            def _(g):
                for b in range(NB):
                    n32 = len([i for i in range(NT) if i % NB == b])
                    if lnx32_hbm is not None:
                        g.wait_ge(s_st32[b].h, s_st32[b].tgt(16 * n32))
                    if lnxbf_hbm is not None:
                        g.wait_ge(s_stbf[b].h, s_stbf[b].tgt(16 * n32))
                if lnx32_hbm is None and lnxbf_hbm is None:
                    g.wait_ge(s_nrm.h, s_nrm.tgt(NT))
        ctx.barrier()
    return


def emit_tok2feat(ctx, consts, src_hbm, row0, nrows, chunks, col0, tag):
    nc = ctx.nc
    ntile = nrows // 128
    NB = 2
    with ctx.frame():
        xt = [ctx.sbf(tag + "x", [128, DM], BF16) for _ in range(NB)]
        tp = [ctx.pb[i][:, 0:128] for i in range(2)]
        s_ld = [ctx.sem(f"SEMFAM_ld{b}") for b in range(NB)]
        s_tp = ctx.sem("SEMFAM_tp")
        s_ev = ctx.sem("SEMFAM_ev")

        with nc.Block() as blk:
            @blk.sync
            def _(s):
                for i in range(ntile):
                    b = i % NB
                    if i >= NB:
                        s.wait_ge(s_tp.h, s_tp.tgt(NDC * (i - NB + 1)))
                    s_ld[b].inc(s.dma_start(
                        out=xt[b][:],
                        in_=src_hbm[row0 + i * 128: row0 + (i + 1) * 128, :]), 16)

            @blk.tensor
            def _(t):
                k = 0
                for i in range(ntile):
                    b = i % NB
                    t.wait_ge(s_ld[b].h, s_ld[b].tgt(16 * (i // NB + 1)))
                    for c in range(NDC):
                        pbk = k % 2
                        if k >= 2:
                            t.wait_ge(s_ev.h, s_ev.tgt(k - 1))
                        s_tp.inc(t.transpose(tp[pbk],
                                             xt[b][:, c * 128:(c + 1) * 128],
                                             consts["ident_bf"][:]))
                        k += 1

            @blk.vector
            def _(v):
                k = 0
                for i in range(ntile):
                    for c in range(NDC):
                        pbk = k % 2
                        v.wait_ge(s_tp.h, s_tp.tgt(k + 1))
                        s_ev.inc(v.tensor_copy(
                            chunks[c][:, col0 + i * 128: col0 + (i + 1) * 128],
                            tp[pbk]))
                        k += 1
        ctx.barrier()
    return


def emit_l1_attn_b(ctx, consts, xin, xout, w_qkv, w_out, b_out, gamma, beta,
                   lnx32_hbm, lnxbf_hbm, tag="l1b"):
    nc = ctx.nc
    KEPS = 1e-6
    NQB = S // 512

    emit_ln(ctx, consts, xin, lnx32_hbm=lnx32_hbm, lnxbf_hbm=lnxbf_hbm,
            tag=tag + "ln")

    with ctx.frame():
        wqkv = [ctx.sbf(tag + "wqkv", [128, 3 * DM], BF16) for _ in range(NDC)]
        wout = [ctx.sbf(tag + "wout", [128, DM], BF16) for _ in range(NDC)]
        bout_r = ctx.sbf(tag + "boutr", [1, DM], BF16)
        gam_r = ctx.sbf(tag + "gamr", [1, DM], F32)
        bet_r = ctx.sbf(tag + "betr", [1, DM], F32)
        gam_bc = ctx.sbf(tag + "gbc", [128, DM], F32)
        bet_bc = ctx.sbf(tag + "bbc", [128, DM], F32)
        p_gb = ctx.pf[0]

        s_wld = ctx.sem("SEMFAM_wld")
        s_gbt = ctx.sem("SEMFAM_gbt")
        s_gbe = ctx.sem("SEMFAM_gbe")
        with nc.Block() as blk:
            @blk.gpsimd
            def _(g):
                for c in range(NDC):
                    s_wld.inc(g.dma_start(
                        out=wqkv[c][:], in_=w_qkv[c * 128:(c + 1) * 128, :]), 16)
                    s_wld.inc(g.dma_start(
                        out=wout[c][:], in_=w_out[c * 128:(c + 1) * 128, :]), 16)
                s_wld.inc(g.dma_start(out=bout_r[:], in_=b_out[:]), 16)
                s_wld.inc(g.dma_start(out=gam_r[:], in_=gamma[:]), 16)
                s_wld.inc(g.dma_start(out=bet_r[:], in_=beta[:]), 16)

            @blk.tensor
            def _(t):
                t.wait_ge(s_wld.h, s_wld.tgt(s_wld.v))
                for j, row in ((0, gam_r), (1, bet_r)):
                    for half in range(2):
                        k = 2 * j + half
                        if k >= 1:
                            t.wait_ge(s_gbe.h, s_gbe.tgt(k))
                        s_gbt.inc(t.matmul(
                            p_gb[:, :], consts["ones_row_f32"][:],
                            row[0:1, half * 512:(half + 1) * 512],
                            start=True, stop=True))

            @blk.vector
            def _(v):
                for j, dst in ((0, gam_bc), (1, bet_bc)):
                    for half in range(2):
                        k = 2 * j + half
                        v.wait_ge(s_gbt.h, s_gbt.tgt(k + 1))
                        s_gbe.inc(v.tensor_copy(
                            dst[:, half * 512:(half + 1) * 512], p_gb[:, :]))
        ctx.barrier()

        for r in range(BL):
            ttag = f"{tag}r{r}"
            with ctx.frame():
                lnxf = [ctx.sbf(ttag + "lnxf", [128, S], BF16)
                        for _ in range(NDC)]
                attn_f = [ctx.sbf(ttag + "af", [128, S], BF16)
                          for _ in range(NDC)]

                emit_tok2feat(ctx, consts, lnxbf_hbm, r * S, S, lnxf, 0,
                              ttag + "tf")

                with ctx.frame():
                    v_sb = ctx.sbf(ttag + "vsb", [128, NH * NTS * 65], BF16)
                    qf = ctx.sbf(ttag + "qf", [128, S], BF16)
                    kf = ctx.sbf(ttag + "kf", [128, S], BF16)
                    probs = ctx.sbf(ttag + "pr", [128, NTS * 512], BF16)
                    bcden = ctx.sbf(ttag + "bcd", [64, 512], F32)
                    den_r = ctx.sbf(ttag + "den", [1, 512], F32)
                    vsb_h = v_sb.ap().rearrange("p (h k c) -> p h k c",
                                                k=NTS, c=65)

                    def vslot(h, kt):
                        return (h * NTS + kt) * 65

                    s_vm = ctx.sem("ROWFAM_vm")
                    s_ve = ctx.sem("ROWFAM_ve")
                    s_vi = ctx.sem("ROWFAM_vi")
                    p_v = [ctx.pf[0], ctx.pf[1]]
                    with nc.Block() as blk:
                        @blk.vector
                        def _(v):
                            s_vi.inc(v.memset(v_sb[:], 1.0))
                            v.wait_ge(s_vi.h, s_vi.tgt(1))
                            k = 0
                            for i in range(NTS):
                                for half in range(2):
                                    v.wait_ge(s_vm.h, s_vm.tgt(k + 1))
                                    s_ve.inc(v.tensor_copy(
                                        vsb_h[:, half * 8:(half + 1) * 8,
                                              i, 0:HD],
                                        p_v[k % 2][:, :]))
                                    k += 1

                        @blk.tensor
                        def _(t):
                            t.wait_ge(s_vi.h, s_vi.tgt(1))
                            k = 0
                            for i in range(NTS):
                                for half in range(2):
                                    pbk = k % 2
                                    if k >= 2:
                                        t.wait_ge(s_ve.h, s_ve.tgt(k - 1))
                                    for c in range(NDC):
                                        mm = t.matmul(
                                            p_v[pbk][:, :],
                                            lnxf[c][:, i * 128:(i + 1) * 128],
                                            wqkv[c][:, 2 * DM + half * 512:
                                                    2 * DM + (half + 1) * 512],
                                            start=(c == 0), stop=(c == NDC - 1))
                                    s_vm.inc(mm)
                                    k += 1
                    ctx.barrier()

                    for hp in range(NH // 2):
                        _emit_l1b_attn_pair(ctx, consts, hp, wqkv, lnxf,
                                            qf, kf, v_sb, vslot, probs,
                                            bcden, den_r, attn_f, KEPS, NQB,
                                            tag=f"{ttag}p{hp}")
                    ctx.barrier()

                with ctx.frame():
                    _emit_l1b_stage4(ctx, consts, r, xin, xout, lnx32_hbm,
                                     attn_f, wout, bout_r, gam_bc, bet_bc,
                                     tag=f"{ttag}s4")
                ctx.barrier()
    return


def _emit_l1b_attn_pair(ctx, consts, hp, wqkv, lnxf, qf, kf, v_sb, vslot,
                        probs, bcden, den_r, attn_f, KEPS, NQB, tag):
    nc = ctx.nc
    s_qk = ctx.sem("SEMFAM_qk")
    s_qke = ctx.sem("SEMFAM_qke")
    s_sc = ctx.sem("SEMFAM_sc")
    s_pr = ctx.sem("SEMFAM_pr")
    s_av = ctx.sem("SEMFAM_av")
    s_dn = ctx.sem("SEMFAM_dn")
    s_bc = ctx.sem("SEMFAM_bc")
    s_be = ctx.sem("SEMFAM_be")
    s_ao = ctx.sem("SEMFAM_ao")
    s_vch = ctx.sem("SEMFAM_vch")
    p_sc = [ctx.pf[0], ctx.pf[1]]
    p_av = ctx.pf[2]
    p_qk = ctx.pf[3]
    p_bc = ctx.pf[4][0:64, :]

    with nc.Block() as blk:
        @blk.tensor
        def _(t):
            for j in range(2):
                for tb in range(NQB):
                    g = j * NQB + tb
                    t.wait_ge(s_qke.h, s_qke.tgt(g))
                    for c in range(NDC):
                        mm = t.matmul(p_qk[:, :],
                                      wqkv[c][:, j * DM + hp * 128:
                                              j * DM + hp * 128 + 128],
                                      lnxf[c][:, tb * 512:(tb + 1) * 512],
                                      start=(c == 0), stop=(c == NDC - 1))
                    s_qk.inc(mm)
            t.wait_ge(s_qke.h, s_qke.tgt(2 * NQB))
            for hh in range(2):
                h = 2 * hp + hh
                hb = hh * 64
                for qb in range(NQB):
                    idx = hh * NQB + qb
                    base_it = idx * NTS
                    for kt in range(NTS):
                        it = base_it + kt
                        t.wait_ge(s_pr.h, s_pr.tgt(max(it - 1, 0)))
                        s_sc.inc(t.matmul(
                            p_sc[it % 2][:, :],
                            kf[hb:hb + 64, kt * 128:(kt + 1) * 128],
                            qf[hb:hb + 64, qb * 512:(qb + 1) * 512],
                            start=True, stop=True))
                    t.wait_ge(s_ao.h, s_ao.tgt(idx))
                    for kt in range(NTS):
                        t.wait_ge(s_pr.h, s_pr.tgt(base_it + kt + 1))
                        mm = t.matmul(p_av[0:65, :],
                                      v_sb[:, vslot(h, kt):vslot(h, kt) + 65],
                                      probs[:, kt * 512:(kt + 1) * 512],
                                      start=(kt == 0), stop=(kt == NTS - 1))
                    s_av.inc(mm)
                    t.wait_ge(s_dn.h, s_dn.tgt(idx + 1))
                    t.wait_ge(s_be.h, s_be.tgt(idx))
                    s_bc.inc(t.matmul(p_bc, consts["ones_row_f32"][0:1, 0:64],
                                      den_r[:, :], start=True, stop=True))

        @blk.scalar
        def _(sc):
            sc.wait_ge(s_sc.h, s_sc.tgt(0))
            for j, dst in ((0, qf), (1, kf)):
                for tb in range(NQB):
                    g = j * NQB + tb
                    sc.wait_ge(s_qk.h, s_qk.tgt(g + 1))
                    s_qke.inc(sc.copy(dst[:, tb * 512:(tb + 1) * 512],
                                      p_qk[:, :]))
            for hh in range(2):
                for qb in range(NQB):
                    idx = hh * NQB + qb
                    base_it = idx * NTS
                    for kt in range(NTS):
                        it = base_it + kt
                        if kt == 0:
                            sc.wait_ge(s_av.h, s_av.tgt(idx))
                        sc.wait_ge(s_sc.h, s_sc.tgt(it + 1))
                        s_pr.inc(sc.activation(
                            probs[:, kt * 512:(kt + 1) * 512],
                            p_sc[it % 2][:, :], AF.Relu, scale=0.125))

        @blk.vector
        def _(v):
            for hh in range(2):
                for qb in range(NQB):
                    idx = hh * NQB + qb
                    v.wait_ge(s_av.h, s_av.tgt(idx + 1))
                    _chain(v, s_vch, v.tensor_scalar_add(den_r[:, :],
                                                         p_av[64:65, :], KEPS))
                    s_dn.inc(v.reciprocal(den_r[:, :], den_r[:, :]))
                    v.wait_ge(s_bc.h, s_bc.tgt(idx + 1))
                    s_be.inc(v.tensor_copy(bcden[:, :], p_bc))
                    v.wait_ge(s_be.h, s_be.tgt(idx + 1))
                    s_ao.inc(v.scalar_tensor_tensor(
                        attn_f[hp][hh * 64:hh * 64 + 64,
                                   qb * 512:(qb + 1) * 512],
                        p_av[0:64, :], 1.0, bcden[:, :],
                        ALU.mult, ALU.mult))
    return


def _emit_l1b_stage4(ctx, consts, r, xin, xout, lnx32_hbm, attn_f, wout,
                     bout_r, gam_bc, bet_bc, tag):
    nc = ctx.nc
    NB = 2
    yt = [ctx.sbf(tag + "y", [128, DM], F32) for _ in range(NB)]
    xt = [ctx.sbf(tag + "x", [128, DM], F32) for _ in range(NB)]
    sq = ctx.sbf(tag + "sq", [128, DM], F32)
    stt = _alloc_ln_stats(ctx, tag)
    p_o = [ctx.pf[0], ctx.pf[1]]

    s_ldy = [ctx.sem(f"SEMFAM_ldy{b}") for b in range(NB)]
    s_ldx = [ctx.sem(f"SEMFAM_ldx{b}") for b in range(NB)]
    s_mm = ctx.sem("SEMFAM_mm")
    s_y = ctx.sem("SEMFAM_y")
    s_sum = ctx.sem("SEMFAM_sum")
    s_sqa = ctx.sem("SEMFAM_sqa")
    s_rst = ctx.sem("SEMFAM_rst")
    s_out = ctx.sem("SEMFAM_out")
    s_ch = ctx.sem("SEMFAM_ch")
    s_vch = ctx.sem("SEMFAM_vch")
    s_st = [ctx.sem(f"SEMFAM_st{b}") for b in range(NB)]

    with nc.Block() as blk:
        @blk.sync
        def _(s):
            for i in range(NTS):
                b = i % NB
                gi = r * NTS + i
                if i >= NB:
                    s.wait_ge(s_out.h, s_out.tgt(i - NB + 1))
                    s.wait_ge(s_sqa.h, s_sqa.tgt(i - NB + 1))
                    s.wait_ge(s_st[b].h, s_st[b].tgt(16 * (i // NB)))
                s_ldy[b].inc(s.dma_start(
                    out=yt[b][:], in_=lnx32_hbm[gi * 128:(gi + 1) * 128, :]), 16)
                s_ldx[b].inc(s.dma_start(
                    out=xt[b][:], in_=xin[gi * 128:(gi + 1) * 128, :]), 16)
                s.wait_ge(s_out.h, s_out.tgt(i + 1))
                s_st[b].inc(s.dma_start(out=xout[gi * 128:(gi + 1) * 128, :],
                                        in_=xt[b][:]), 16)

        @blk.tensor
        def _(t):
            for i in range(NTS):
                for half in range(2):
                    k = 2 * i + half
                    if k >= 2:
                        t.wait_ge(s_y.h, s_y.tgt(k - 1))
                    for c in range(NDC):
                        t.matmul(p_o[k % 2][:, :],
                                 attn_f[c][:, i * 128:(i + 1) * 128],
                                 wout[c][:, half * 512:(half + 1) * 512],
                                 start=(c == 0), stop=False)
                    s_mm.inc(t.matmul(p_o[k % 2][:, :], consts["ones_row_bf"][:],
                                      bout_r[0:1, half * 512:(half + 1) * 512],
                                      start=False, stop=True))

        @blk.vector
        def _(v):
            for i in range(NTS):
                b, p = i % NB, i % 2
                v.wait_ge(s_ldy[b].h, s_ldy[b].tgt(16 * (i // NB + 1)))
                for half in range(2):
                    k = 2 * i + half
                    v.wait_ge(s_mm.h, s_mm.tgt(k + 1))
                    s_y.inc(v.tensor_tensor(
                        yt[b][:, half * 512:(half + 1) * 512],
                        yt[b][:, half * 512:(half + 1) * 512],
                        p_o[k % 2][:, :], ALU.add))
                v.wait_ge(s_y.h, s_y.tgt(2 * (i + 1)))
                if i >= 2:
                    v.wait_ge(s_rst.h, s_rst.tgt(i - 1))
                s_sum.inc(v.reduce_sum(stt["st_sum"][p][:], yt[b][:], axis=AX.X))
                v.wait_ge(s_rst.h, s_rst.tgt(i + 1))
                v.wait_ge(s_sqa.h, s_sqa.tgt(i + 1))
                _chain(v, s_vch, v.tensor_scalar(yt[b][:], yt[b][:],
                                                 stt["st_mean"][p][:],
                                                 stt["st_rstd"][p][:],
                                                 ALU.subtract, ALU.mult))
                _chain(v, s_vch, v.tensor_tensor(yt[b][:], yt[b][:], gam_bc[:],
                                                 ALU.mult))
                _chain(v, s_vch, v.tensor_tensor(yt[b][:], yt[b][:], bet_bc[:],
                                                 ALU.add))
                v.wait_ge(s_ldx[b].h, s_ldx[b].tgt(16 * (i // NB + 1)))
                s_out.inc(v.tensor_tensor(xt[b][:], xt[b][:], yt[b][:], ALU.add))

        @blk.scalar
        def _(sc):
            for i in range(NTS):
                b, p = i % NB, i % 2
                sc.wait_ge(s_y.h, s_y.tgt(2 * (i + 1)))
                if i >= 2:
                    sc.wait_ge(s_out.h, s_out.tgt(i - 1))
                s_rst.inc(_ln_stats_scalar(sc, p, sq, yt[b], stt,
                                           s_sqa, s_sum, s_ch, i))

        @blk.gpsimd
        def _(g):
            for b in range(NB):
                nb = len([i for i in range(NTS) if i % NB == b])
                g.wait_ge(s_st[b].h, s_st[b].tgt(16 * nb))
    return


def build_l1_attn_b():
    nc = bass.Bass("TRN2", target_bir_lowering=False, debug=False,
                   dynamic_dma_scratch_size=8192)
    ctx = Ctx(nc)
    xin = nc.dram_tensor("xin", [T, DM], F32, kind="ExternalInput").ap()
    w_qkv = nc.dram_tensor("w_qkv", [DM, 3 * DM], F32, kind="ExternalInput").ap()
    w_out = nc.dram_tensor("w_out", [DM, DM], F32, kind="ExternalInput").ap()
    b_out = nc.dram_tensor("b_out", [1, DM], F32, kind="ExternalInput").ap()
    gamma = nc.dram_tensor("gamma", [1, DM], F32, kind="ExternalInput").ap()
    beta = nc.dram_tensor("beta", [1, DM], F32, kind="ExternalInput").ap()
    xout = nc.dram_tensor("xout", [T, DM], F32, kind="ExternalOutput").ap()
    lnx32 = nc.dram_tensor("lnx32", [T, DM], F32, kind="Internal").ap()
    lnxbf = nc.dram_tensor("lnxbf", [T, DM], BF16, kind="Internal").ap()

    consts = make_consts(ctx)
    emit_l1_attn_b(ctx, consts, xin, xout, w_qkv, w_out, b_out, gamma, beta,
                   lnx32, lnxbf)
    return nc


def _emit_outproj_residual(ctx, consts, r, xin, xout, attn_f, wo_dram, tag):
    nc = ctx.nc
    NB = 2
    xt = [ctx.sbf(tag + "x", [128, DM], F32) for _ in range(NB)]
    wo_sb = [ctx.sbf(tag + "wo", [128, DM], BF16) for _ in range(NDC)]
    wo_f32 = ctx.sbf(tag + "wof", [128, DM], F32)
    p_o = [ctx.pf[0], ctx.pf[1]]

    s_wld2 = [ctx.sem(f"SEMFAM_wld2{c}") for c in range(NDC)]
    s_woc = ctx.sem("SEMFAM_woc")
    s_ldx = [ctx.sem(f"SEMFAM_ldx{b}") for b in range(NB)]
    s_mm = ctx.sem("SEMFAM_mm")
    s_y = ctx.sem("SEMFAM_y")
    s_out = ctx.sem("SEMFAM_out")
    s_st = [ctx.sem(f"SEMFAM_st{b}") for b in range(NB)]

    with nc.Block() as blk:
        @blk.sync
        def _(s):
            for c in range(NDC):
                if c >= 1:
                    s.wait_ge(s_woc.h, s_woc.tgt(c))
                s_wld2[c].inc(s.dma_start(
                    out=wo_f32[:], in_=wo_dram[c * 128:(c + 1) * 128, :]), 16)
            for i in range(NTS):
                b = i % NB
                gi = r * NTS + i
                if i >= NB:
                    s.wait_ge(s_out.h, s_out.tgt(i - NB + 1))
                    s.wait_ge(s_st[b].h, s_st[b].tgt(16 * (i // NB)))
                s_ldx[b].inc(s.dma_start(
                    out=xt[b][:], in_=xin[gi * 128:(gi + 1) * 128, :]), 16)
                s.wait_ge(s_out.h, s_out.tgt(i + 1))
                s_st[b].inc(s.dma_start(out=xout[gi * 128:(gi + 1) * 128, :],
                                        in_=xt[b][:]), 16)

        @blk.scalar
        def _(sc):
            for c in range(NDC):
                sc.wait_ge(s_wld2[c].h, s_wld2[c].tgt(16))
                s_woc.inc(sc.copy(wo_sb[c][:], wo_f32[:]))

        @blk.tensor
        def _(t):
            t.wait_ge(s_woc.h, s_woc.tgt(NDC))
            for i in range(NTS):
                for half in range(2):
                    k = 2 * i + half
                    t.wait_ge(s_y.h, s_y.tgt(max(k - 1, 0)))
                    for c in range(NDC):
                        mm = t.matmul(p_o[k % 2][:, :],
                                      attn_f[c][:, i * 128:(i + 1) * 128],
                                      wo_sb[c][:, half * 512:(half + 1) * 512],
                                      start=(c == 0), stop=(c == NDC - 1))
                    s_mm.inc(mm)

        @blk.vector
        def _(v):
            for i in range(NTS):
                b = i % NB
                v.wait_ge(s_ldx[b].h, s_ldx[b].tgt(16 * (i // NB + 1)))
                for half in range(2):
                    k = 2 * i + half
                    v.wait_ge(s_mm.h, s_mm.tgt(k + 1))
                    ins = v.tensor_tensor(
                        xt[b][:, half * 512:(half + 1) * 512],
                        xt[b][:, half * 512:(half + 1) * 512],
                        p_o[k % 2][:, :], ALU.add)
                    s_y.inc(ins)
                v.wait_ge(s_y.h, s_y.tgt(2 * (i + 1)))
                s_out.inc(v.nop())

        @blk.scalar
        def _(sc):
            for b in range(NB):
                nb = len([i for i in range(NTS) if i % NB == b])
                sc.wait_ge(s_st[b].h, s_st[b].tgt(16 * nb))
    return


L1A_PAGE = 256
L1A_LAT = 512
NLC = L1A_LAT // 128


def _rope_tables_l1a():
    inv = 1.0 / (10000.0 ** (np.arange(0, HD, 2, dtype=np.float64) / HD))
    ang = np.outer(inv, np.arange(S, dtype=np.float64))
    cos, sin = np.cos(ang), np.sin(ang)
    cs = np.tile(cos, (4, 1)).astype(np.float32)
    sns = np.concatenate([-sin, sin, -sin, sin], axis=0).astype(np.float32)
    return cs, sns


def emit_l1_attn_a(ctx, consts, xin, xout, wq, w_down, wk_up, wv_up, wo,
                   lnxbf_hbm, tag="l1a"):
    nc = ctx.nc
    emit_ln(ctx, consts, xin, lnxbf_hbm=lnxbf_hbm, tag=tag + "ln")

    cos_np, sin_np = _rope_tables_l1a()
    d_cos = ctx.const("l1a_cos", _bf(cos_np))
    d_sin = ctx.const("l1a_sin", _bf(sin_np))

    with ctx.frame():
        wq_sb = [ctx.sbf(tag + "wq", [128, DM], BF16) for _ in range(NDC)]
        wdn_sb = [ctx.sbf(tag + "wdn", [128, L1A_LAT], BF16) for _ in range(NDC)]
        wku_sb = [ctx.sbf(tag + "wku", [128, DM], BF16) for _ in range(NLC)]
        wvu_sb = [ctx.sbf(tag + "wvu", [128, DM], BF16) for _ in range(NLC)]
        cos_sb = ctx.sbf(tag + "cos", [128, S], BF16)
        sin_sb = ctx.sbf(tag + "sin", [128, S], BF16)

        s_wld = ctx.sem("SEMFAM_wld")
        with nc.Block() as blk:
            @blk.gpsimd
            def _(g):
                for c in range(NDC):
                    s_wld.inc(g.dma_start(out=wq_sb[c][:],
                                          in_=wq[c * 128:(c + 1) * 128, :]), 16)
                    s_wld.inc(g.dma_start(out=wdn_sb[c][:],
                                          in_=w_down[c * 128:(c + 1) * 128, :]), 16)
                for c in range(NLC):
                    s_wld.inc(g.dma_start(out=wku_sb[c][:],
                                          in_=wk_up[c * 128:(c + 1) * 128, :]), 16)
                    s_wld.inc(g.dma_start(out=wvu_sb[c][:],
                                          in_=wv_up[c * 128:(c + 1) * 128, :]), 16)
                s_wld.inc(g.dma_start(out=cos_sb[:], in_=d_cos[:]), 16)
                s_wld.inc(g.dma_start(out=sin_sb[:], in_=d_sin[:]), 16)
                g.wait_ge(s_wld.h, s_wld.tgt(s_wld.v - s_wld.b))
        ctx.barrier()

        for r in range(BL):
            ttag = f"{tag}r{r}"
            with ctx.frame():
                lnxf = [ctx.sbf(ttag + "lnxf", [128, S], BF16)
                        for _ in range(NDC)]
                attn_f = [ctx.sbf(ttag + "af", [128, S], BF16)
                          for _ in range(NDC)]

                emit_tok2feat(ctx, consts, lnxbf_hbm, r * S, S, lnxf, 0,
                              ttag + "tf")

                with ctx.frame():
                    lat_f = [ctx.sbf(ttag + "lat", [128, S], BF16)
                             for _ in range(NLC)]
                    s_lm = ctx.sem("ROWFAM_vm")
                    s_le = ctx.sem("ROWFAM_ve")
                    p_l = [ctx.pf[0], ctx.pf[1]]
                    with nc.Block() as blk:
                        @blk.tensor
                        def _(t):
                            k = 0
                            for lc in range(NLC):
                                for tb in range(S // 512):
                                    pbk = k % 2
                                    t.wait_ge(s_le.h, s_le.tgt(max(k - 1, 0)))
                                    for c in range(NDC):
                                        mm = t.matmul(
                                            p_l[pbk][:, :],
                                            wdn_sb[c][:, lc * 128:(lc + 1) * 128],
                                            lnxf[c][:, tb * 512:(tb + 1) * 512],
                                            start=(c == 0), stop=(c == NDC - 1))
                                    s_lm.inc(mm)
                                    k += 1

                        @blk.vector
                        def _(v):
                            k = 0
                            for lc in range(NLC):
                                for tb in range(S // 512):
                                    v.wait_ge(s_lm.h, s_lm.tgt(k + 1))
                                    s_le.inc(v.tensor_copy(
                                        lat_f[lc][:, tb * 512:(tb + 1) * 512],
                                        p_l[k % 2][:, :]))
                                    k += 1
                    ctx.barrier()

                    v_sb = ctx.sbf(ttag + "vsb", [128, NH * NTS * 65], BF16)
                    vsb_h = v_sb.ap().rearrange("p (h k c) -> p h k c",
                                                k=NTS, c=65)
                    s_vm = ctx.sem("ROWFAM_vm2")
                    s_ve = ctx.sem("ROWFAM_ve2")
                    s_vi = ctx.sem("ROWFAM_vi")
                    p_v = [ctx.pf[0], ctx.pf[1]]
                    with nc.Block() as blk:
                        @blk.vector
                        def _(v):
                            s_vi.inc(v.memset(v_sb[:], 1.0))
                            v.wait_ge(s_vi.h, s_vi.tgt(1))
                            k = 0
                            for i in range(NTS):
                                for half in range(2):
                                    v.wait_ge(s_vm.h, s_vm.tgt(k + 1))
                                    s_ve.inc(v.tensor_copy(
                                        vsb_h[:, half * 8:(half + 1) * 8, i, 0:HD],
                                        p_v[k % 2][:, :]))
                                    k += 1

                        @blk.tensor
                        def _(t):
                            t.wait_ge(s_vi.h, s_vi.tgt(1))
                            k = 0
                            for i in range(NTS):
                                for half in range(2):
                                    pbk = k % 2
                                    t.wait_ge(s_ve.h, s_ve.tgt(max(k - 1, 0)))
                                    for c in range(NLC):
                                        mm = t.matmul(
                                            p_v[pbk][:, :],
                                            lat_f[c][:, i * 128:(i + 1) * 128],
                                            wvu_sb[c][:, half * 512:
                                                      (half + 1) * 512],
                                            start=(c == 0), stop=(c == NLC - 1))
                                    s_vm.inc(mm)
                                    k += 1
                    ctx.barrier()

                    for hp in range(NH // 2):
                        with ctx.frame():
                            _emit_l1a_attn_pair(ctx, consts, hp, wq_sb, wku_sb,
                                                lnxf, lat_f, v_sb, cos_sb,
                                                sin_sb, attn_f,
                                                tag=f"{ttag}p{hp}")
                    ctx.barrier()

                with ctx.frame():
                    _emit_outproj_residual(ctx, consts, r, xin, xout, attn_f,
                                           wo, tag=f"{ttag}s4")
                ctx.barrier()
    return


def _emit_l1a_attn_pair(ctx, consts, hp, wq_sb, wku_sb, lnxf, lat_f, v_sb,
                        cos_sb, sin_sb, attn_f, tag):
    nc = ctx.nc
    NPAGE = S // L1A_PAGE
    s_qk = ctx.sem("SEMFAM_qk")
    s_qke = ctx.sem("SEMFAM_qke")
    s_rp = ctx.sem("SEMFAM_rp")
    s_sq2 = ctx.sem("SEMFAM_sq2")
    s_nm = ctx.sem("SEMFAM_nm")
    s_ne = ctx.sem("SEMFAM_ne")
    s_cbv = ctx.sem("SEMFAM_cbv")
    s_cbs = ctx.sem("SEMFAM_cbs")
    s_cbm = ctx.sem("SEMFAM_cbm")
    s_cb = ctx.sem("SEMFAM_cb")
    s_sc = ctx.sem("SEMFAM_sc")
    s_pr = ctx.sem("SEMFAM_pr")
    s_av = ctx.sem("SEMFAM_av")
    s_dn = ctx.sem("SEMFAM_dn")
    s_bc = ctx.sem("SEMFAM_bc")
    s_be = ctx.sem("SEMFAM_be")
    s_ao = ctx.sem("SEMFAM_ao")
    s_vch = ctx.sem("SEMFAM_vch")
    s_sch = ctx.sem("SEMFAM_ch")
    p_sc = [ctx.pf[0], ctx.pf[1]]
    p_av = ctx.pf[2]
    p_qk = ctx.pf[3]
    p_bc = ctx.pf[4]
    p_nrm = ctx.pf[5]

    qraw = ctx.sbf(tag + "qraw", [128, S], BF16)
    kraw = qraw
    qf = ctx.sbf(tag + "qf", [128, S], BF16)
    kf = ctx.sbf(tag + "kf", [128, S], BF16)
    rtmp = ctx.sbf(tag + "rtmp", [128, S], BF16)
    sqs = ctx.sbf(tag + "sqs", [128, S], BF16)
    probs = ctx.sbf(tag + "pr", [128, NTS * 256], BF16)
    bcden = ctx.sbf(tag + "bcd", [64, 256], F32)
    den_r = ctx.sbf(tag + "den", [1, 256], F32)
    nrm_r = ctx.sbf(tag + "nrm", [1, S], BF16)
    mx = ctx.sbf(tag + "mx", [1, 2], F32)
    cb = ctx.sbf(tag + "cb", [128, 1], F32)
    cbrow = ctx.sbf(tag + "cbr", [1, 1], F32)

    probs_h = probs.ap().rearrange("p (k q) -> p k q", q=256)

    def vslot(h, kt):
        return (h * NTS + kt) * 65

    with nc.Block() as blk:
        @blk.tensor
        def _(t):
            for j in range(2):
                wsb = wq_sb if j == 0 else wku_sb
                srcs = lnxf if j == 0 else lat_f
                nsrc = NDC if j == 0 else NLC
                for tb in range(S // 512):
                    g = j * (S // 512) + tb
                    t.wait_ge(s_qke.h, s_qke.tgt(g))
                    for c in range(nsrc):
                        mm = t.matmul(p_qk[:, :],
                                      wsb[c][:, hp * 128:hp * 128 + 128],
                                      srcs[c][:, tb * 512:(tb + 1) * 512],
                                      start=(c == 0), stop=(c == nsrc - 1))
                    s_qk.inc(mm)
            for j in range(2):
                t.wait_ge(s_sq2.h, s_sq2.tgt(j + 1))
                for tb in range(S // 512):
                    t.wait_ge(s_ne.h, s_ne.tgt(j * 4 + tb))
                    s_nm.inc(t.matmul(p_nrm[0:1, :],
                                      consts["ones_col_bf"][:, 0:1],
                                      sqs[:, tb * 512:(tb + 1) * 512],
                                      start=True, stop=True))
            t.wait_ge(s_cbs.h, s_cbs.tgt(1))
            s_cbm.inc(t.matmul(p_bc[:, 0:1], consts["ones_row_f32"][:],
                               cbrow[0:1, 0:1], start=True, stop=True))
            t.wait_ge(s_rp.h, s_rp.tgt(2))
            t.wait_ge(s_cb.h, s_cb.tgt(1))
            it_sc = 0
            idx = 0
            for hh in range(2):
                h = 2 * hp + hh
                hb = hh * 64
                for pg in range(NPAGE):
                    nkt = 2 * (pg + 1)
                    t.wait_ge(s_ao.h, s_ao.tgt(idx))
                    for kt in range(nkt):
                        t.wait_ge(s_pr.h, s_pr.tgt(max(it_sc - 1, 0)))
                        s_sc.inc(t.matmul(
                            p_sc[it_sc % 2][:, 0:256],
                            kf[hb:hb + 64, kt * 128:(kt + 1) * 128],
                            qf[hb:hb + 64, pg * 256:(pg + 1) * 256],
                            start=True, stop=True))
                        it_sc += 1
                    for kt in range(nkt):
                        t.wait_ge(s_pr.h, s_pr.tgt(it_sc - nkt + kt + 1))
                        mm = t.matmul(p_av[0:65, 0:256],
                                      v_sb[:, vslot(h, kt):vslot(h, kt) + 65],
                                      probs_h[:, kt, :],
                                      start=(kt == 0), stop=(kt == nkt - 1))
                    s_av.inc(mm)
                    t.wait_ge(s_dn.h, s_dn.tgt(idx + 1))
                    t.wait_ge(s_be.h, s_be.tgt(idx))
                    s_bc.inc(t.matmul(p_bc[0:64, 0:256],
                                      consts["ones_row_f32"][0:1, 0:64],
                                      den_r[:, :], start=True, stop=True))
                    idx += 1

        @blk.scalar
        def _(sc):
            sc.wait_ge(s_sc.h, s_sc.tgt(0))
            for j, dst in ((0, qraw), (1, kraw)):
                if j == 1:
                    sc.wait_ge(s_rp.h, s_rp.tgt(1))
                for tb in range(S // 512):
                    g = j * (S // 512) + tb
                    sc.wait_ge(s_qk.h, s_qk.tgt(g + 1))
                    s_qke.inc(sc.copy(dst[:, tb * 512:(tb + 1) * 512],
                                      p_qk[:, :]))
            for j, srcb in ((0, qf), (1, kf)):
                sc.wait_ge(s_rp.h, s_rp.tgt(j + 1))
                if j == 1:
                    sc.wait_ge(s_nm.h, s_nm.tgt(4))
                s_sq2.inc(sc.activation(sqs[:], srcb[:], AF.Square))
            sc.wait_ge(s_cbv.h, s_cbv.tgt(1))
            _chain(sc, s_sch, sc.activation(cbrow[0:1, 0:1], cbrow[0:1, 0:1],
                                            AF.Sqrt))
            s_cbs.inc(sc.mul(cbrow[0:1, 0:1], cbrow[0:1, 0:1], -0.125))
            sc.wait_ge(s_cbm.h, s_cbm.tgt(1))
            s_cb.inc(sc.copy(cb[:], p_bc[:, 0:1]))
            sc.wait_ge(s_cb.h, s_cb.tgt(1))
            it_sc = 0
            idx = 0
            for hh in range(2):
                for pg in range(NPAGE):
                    nkt = 2 * (pg + 1)
                    sc.wait_ge(s_av.h, s_av.tgt(idx))
                    for kt in range(nkt):
                        sc.wait_ge(s_sc.h, s_sc.tgt(it_sc + 1))
                        s_pr.inc(sc.activation(
                            probs_h[:, kt, :],
                            p_sc[it_sc % 2][:, 0:256], AF.Exp,
                            bias=cb[:], scale=0.125))
                        it_sc += 1
                    idx += 1

        @blk.vector
        def _(v):
            for j in range(2):
                rawb = qraw if j == 0 else kraw
                dstb = qf if j == 0 else kf
                v.wait_ge(s_qke.h, s_qke.tgt((j + 1) * (S // 512)))
                _chain(v, s_vch, v.tensor_tensor(dstb[:], rawb[:], cos_sb[:],
                                                 ALU.mult))
                for blkk in range(4):
                    sb32 = blkk * 32
                    db32 = (blkk ^ 1) * 32
                    _chain(v, s_vch, v.tensor_copy(
                        rtmp[db32:db32 + 32, :], rawb[sb32:sb32 + 32, :]))
                _chain(v, s_vch, v.tensor_tensor(rtmp[:], rtmp[:], sin_sb[:],
                                                 ALU.mult))
                _chain(v, s_vch, v.tensor_tensor(dstb[:], dstb[:], rtmp[:],
                                                 ALU.add))
                s_rp.inc(v.nop())
            for j in range(2):
                for tb in range(S // 512):
                    v.wait_ge(s_nm.h, s_nm.tgt(j * 4 + tb + 1))
                    s_ne.inc(v.tensor_copy(
                        nrm_r[0:1, tb * 512:(tb + 1) * 512], p_nrm[0:1, :]))
                v.wait_ge(s_ne.h, s_ne.tgt((j + 1) * 4))
                _chain(v, s_vch, v.reduce_max(mx[0:1, j:j + 1], nrm_r[0:1, :],
                                              axis=AX.X))
            s_cbv.inc(v.tensor_tensor(cbrow[0:1, 0:1], mx[0:1, 0:1],
                                      mx[0:1, 1:2], ALU.mult))
            idx = 0
            for hh in range(2):
                for pg in range(NPAGE):
                    v.wait_ge(s_av.h, s_av.tgt(idx + 1))
                    _chain(v, s_vch, v.tensor_copy(den_r[:, :],
                                                   p_av[64:65, 0:256]))
                    s_dn.inc(v.reciprocal(den_r[:, :], den_r[:, :]))
                    v.wait_ge(s_bc.h, s_bc.tgt(idx + 1))
                    s_be.inc(v.tensor_copy(bcden[:, 0:256], p_bc[0:64, 0:256]))
                    v.wait_ge(s_be.h, s_be.tgt(idx + 1))
                    s_ao.inc(v.scalar_tensor_tensor(
                        attn_f[hp][hh * 64:hh * 64 + 64,
                                   pg * 256:(pg + 1) * 256],
                        p_av[0:64, 0:256], 1.0, bcden[:, 0:256],
                        ALU.mult, ALU.mult))
                    idx += 1
    return


def build_l1_attn_a():
    nc = bass.Bass("TRN2", target_bir_lowering=False, debug=False,
                   dynamic_dma_scratch_size=8192)
    ctx = Ctx(nc)
    xin = nc.dram_tensor("xin", [T, DM], F32, kind="ExternalInput").ap()
    wq = nc.dram_tensor("wq", [DM, DM], F32, kind="ExternalInput").ap()
    w_down = nc.dram_tensor("w_down", [DM, L1A_LAT], F32, kind="ExternalInput").ap()
    wk_up = nc.dram_tensor("wk_up", [L1A_LAT, DM], F32, kind="ExternalInput").ap()
    wv_up = nc.dram_tensor("wv_up", [L1A_LAT, DM], F32, kind="ExternalInput").ap()
    wo = nc.dram_tensor("wo", [DM, DM], F32, kind="ExternalInput").ap()
    xout = nc.dram_tensor("xout", [T, DM], F32, kind="ExternalOutput").ap()
    lnxbf = nc.dram_tensor("lnxbf", [T, DM], BF16, kind="Internal").ap()
    consts = make_consts(ctx)
    emit_l1_attn_a(ctx, consts, xin, xout, wq, w_down, wk_up, wv_up, wo, lnxbf)
    return nc


def emit_l0_attn_a(ctx, consts, xin, xout, wqkv_d, wproj_d, alpha_log,
                   beta_log, lnxbf_hbm, tag="l0a"):
    nc = ctx.nc
    emit_ln(ctx, consts, xin, lnxbf_hbm=lnxbf_hbm, tag=tag + "ln")

    pos_np = np.arange(1, S + 1, dtype=np.float32).reshape(1, S)
    d_pos = ctx.const("l0a_pos", pos_np)
    mask_np = np.triu(np.ones((128, 128), np.float32))
    d_mask = ctx.const("l0a_maskut", _bf(mask_np))

    with ctx.frame():
        wqkv = [ctx.sbf(tag + "wqkv", [128, 3 * DM], BF16) for _ in range(NDC)]
        ab_r = ctx.sbf(tag + "ab", [2, NH], F32)
        al_r = ctx.sbf(tag + "al", [1, NH], F32)
        be_r = ctx.sbf(tag + "be", [1, NH], F32)
        scale_r = ctx.sbf(tag + "scl", [1, NH], F32)
        spt_a = ctx.sbf(tag + "spta", [1, NH], F32)
        spu_a = ctx.sbf(tag + "spua", [1, NH], F32)
        spt_b = ctx.sbf(tag + "sptb", [1, NH], F32)
        spu_b = ctx.sbf(tag + "spub", [1, NH], F32)
        pos_r = ctx.sbf(tag + "pos", [1, S], F32)
        mask_sb = ctx.sbf(tag + "msk", [128, 128], BF16)

        s_wld = ctx.sem("SEMFAM_wld")
        s_sp = ctx.sem("SEMFAM_sp")
        s_spA = ctx.sem("SEMFAM_spA")
        s_vch = ctx.sem("SEMFAM_vch")
        s_sch = ctx.sem("SEMFAM_ch")
        with nc.Block() as blk:
            @blk.gpsimd
            def _(g):
                for c in range(NDC):
                    s_wld.inc(g.dma_start(out=wqkv[c][:],
                                          in_=wqkv_d[c * 128:(c + 1) * 128, :]), 16)
                s_wld.inc(g.dma_start(out=al_r[:], in_=alpha_log[:]), 16)
                s_wld.inc(g.dma_start(out=be_r[:], in_=beta_log[:]), 16)
                s_wld.inc(g.dma_start(out=pos_r[:], in_=d_pos[:]), 16)
                s_wld.inc(g.dma_start(out=mask_sb[:], in_=d_mask[:]), 16)

            @blk.scalar
            def _(sc):
                sc.wait_ge(s_wld.h, s_wld.tgt(s_wld.v - s_wld.b))
                for j, (row, ln1, rel) in ((0, (al_r, spt_a, spu_a)),
                                           (1, (be_r, spt_b, spu_b))):
                    _chain(sc, s_sch, sc.activation(ln1[:], row[:], AF.Abs))
                    _chain(sc, s_sch, sc.activation(ln1[:], ln1[:], AF.Exp,
                                                    scale=-1.0))
                    _chain(sc, s_sch, sc.activation(ln1[:], ln1[:], AF.Ln,
                                                    bias=1.0))
                    _chain(sc, s_sch, sc.activation(rel[:], row[:], AF.Relu))
                s_spA.inc(sc.nop())

            @blk.vector
            def _(v):
                v.wait_ge(s_spA.h, s_spA.tgt(1))
                _chain(v, s_vch, v.tensor_tensor(spu_a[:], spu_a[:], spt_a[:],
                                                 ALU.add))
                _chain(v, s_vch, v.tensor_scalar_add(spu_a[:], spu_a[:], 1e-6))
                _chain(v, s_vch, v.tensor_tensor(spu_b[:], spu_b[:], spt_b[:],
                                                 ALU.add))
                _chain(v, s_vch, v.tensor_scalar_add(spu_b[:], spu_b[:], 1e-6))
                _chain(v, s_vch, v.tensor_tensor(scale_r[:], spu_a[:],
                                                 spu_b[:], ALU.mult))
                s_sp.inc(v.tensor_scalar_mul(scale_r[:], scale_r[:], 8.0))
        ctx.barrier()

        for r in range(BL):
            ttag = f"{tag}r{r}"
            with ctx.frame():
                lnxf = [ctx.sbf(ttag + "lnxf", [128, S], BF16)
                        for _ in range(NDC)]
                attn_f = [ctx.sbf(ttag + "af", [128, S], BF16)
                          for _ in range(NDC)]

                emit_tok2feat(ctx, consts, lnxbf_hbm, r * S, S, lnxf, 0,
                              ttag + "tf")

                with ctx.frame():
                    v_sb = ctx.sbf(ttag + "vsb", [128, NH * NTS * HD], BF16)
                    vsb_h = v_sb.ap().rearrange("p (h k c) -> p h k c",
                                                k=NTS, c=HD)
                    s_vm = ctx.sem("ROWFAM_vm")
                    s_ve = ctx.sem("ROWFAM_ve")
                    p_v = [ctx.pf[0], ctx.pf[1]]
                    with nc.Block() as blk:
                        @blk.vector
                        def _(v):
                            k = 0
                            for i in range(NTS):
                                for half in range(2):
                                    v.wait_ge(s_vm.h, s_vm.tgt(k + 1))
                                    s_ve.inc(v.tensor_copy(
                                        vsb_h[:, half * 8:(half + 1) * 8, i, :],
                                        p_v[k % 2][:, :]))
                                    k += 1

                        @blk.tensor
                        def _(t):
                            k = 0
                            for i in range(NTS):
                                for half in range(2):
                                    pbk = k % 2
                                    t.wait_ge(s_ve.h, s_ve.tgt(max(k - 1, 0)))
                                    for c in range(NDC):
                                        mm = t.matmul(
                                            p_v[pbk][:, :],
                                            lnxf[c][:, i * 128:(i + 1) * 128],
                                            wqkv[c][:, 2 * DM + half * 512:
                                                    2 * DM + (half + 1) * 512],
                                            start=(c == 0), stop=(c == NDC - 1))
                                    s_vm.inc(mm)
                                    k += 1
                    ctx.barrier()

                    for hp in range(NH // 2):
                        with ctx.frame():
                            _emit_l0a_attn_pair(ctx, consts, hp, wqkv, lnxf,
                                                v_sb, mask_sb, pos_r, scale_r,
                                                attn_f, tag=f"{ttag}p{hp}")
                    ctx.barrier()

                with ctx.frame():
                    _emit_outproj_residual(ctx, consts, r, xin, xout, attn_f,
                                           wproj_d, tag=f"{ttag}s4")
                ctx.barrier()
    return


def _emit_l0a_attn_pair(ctx, consts, hp, wqkv, lnxf, v_sb, mask_sb, pos_r,
                        scale_r, attn_f, tag):
    nc = ctx.nc
    s_qk = ctx.sem("SEMFAM_qk")
    s_qke = ctx.sem("SEMFAM_qke")
    s_rd = ctx.sem("SEMFAM_rd")
    s_rdb = ctx.sem("SEMFAM_rdb")
    s_rde = ctx.sem("SEMFAM_rde")
    s_kt = ctx.sem("SEMFAM_kt")
    s_kte = ctx.sem("SEMFAM_kte")
    s_sc = ctx.sem("SEMFAM_sc")
    s_pr = ctx.sem("SEMFAM_pr")
    s_scp = ctx.sem("SEMFAM_scp")
    s_sac = ctx.sem("SEMFAM_sac")
    s_av = ctx.sem("SEMFAM_av")
    s_ao = ctx.sem("SEMFAM_ao")
    s_vch = ctx.sem("SEMFAM_vch")
    p_sc = [ctx.pf[0], ctx.pf[1]]
    p_out = ctx.pf[2]
    p_qk = ctx.pf[3]
    p_bc = ctx.pf[4]
    p_S = ctx.pf[5]
    p_kt = [ctx.pb[0][:, 0:128], ctx.pb[1][:, 0:128]]

    phiq = ctx.sbf(tag + "phiq", [128, S], BF16)
    phik = ctx.sbf(tag + "phik", [128, S], BF16)
    phikt = ctx.sbf(tag + "phikt", [128, NTS * HD], BF16)
    probs = [ctx.sbf(tag + "prb", [128, 128], BF16) for _ in range(2)]
    S_sb = ctx.sbf(tag + "ssb", [128, HD], BF16)
    S_32 = ctx.sbf(tag + "s32", [128, HD], F32)
    rd_r = ctx.sbf(tag + "rd", [1, S], F32)
    bcd = ctx.sbf(tag + "bcd", [64, 128], F32)

    with nc.Block() as blk:
        @blk.tensor
        def _(t):
            for j in range(2):
                for tb in range(S // 512):
                    g = j * (S // 512) + tb
                    t.wait_ge(s_qke.h, s_qke.tgt(g))
                    for c in range(NDC):
                        mm = t.matmul(p_qk[:, :],
                                      wqkv[c][:, j * DM + hp * 128:
                                              j * DM + hp * 128 + 128],
                                      lnxf[c][:, tb * 512:(tb + 1) * 512],
                                      start=(c == 0), stop=(c == NDC - 1))
                    s_qk.inc(mm)
            t.wait_ge(s_qke.h, s_qke.tgt(2 * (S // 512)))
            for hh in range(2):
                h = 2 * hp + hh
                hb = hh * 64
                for i in range(NTS):
                    ii = hh * NTS + i
                    t.wait_ge(s_kte.h, s_kte.tgt(max(ii - 1, 0)))
                    s_kt.inc(t.transpose(p_kt[ii % 2][:, 0:64],
                                         phik[hb:hb + 64,
                                              i * 128:(i + 1) * 128],
                                         consts["ident_bf"][hb:hb + 64,
                                                            hb:hb + 64]))
                for i in range(NTS):
                    ii = hh * NTS + i
                    t.wait_ge(s_pr.h, s_pr.tgt(max(ii - 1, 0)))
                    s_sc.inc(t.matmul(p_sc[ii % 2][:, 0:128],
                                      phik[hb:hb + 64, i * 128:(i + 1) * 128],
                                      phiq[hb:hb + 64, i * 128:(i + 1) * 128],
                                      start=True, stop=True))
                    t.wait_ge(s_rd.h, s_rd.tgt(hh + 1))
                    t.wait_ge(s_rde.h, s_rde.tgt(max(ii, 0)))
                    s_rdb.inc(t.matmul(p_bc[0:64, 0:128],
                                       consts["ones_row_f32"][0:1, 0:64],
                                       rd_r[0:1, i * 128:(i + 1) * 128],
                                       start=True, stop=True))
                    t.wait_ge(s_ao.h, s_ao.tgt(ii))
                    t.wait_ge(s_pr.h, s_pr.tgt(ii + 1))
                    t.wait_ge(s_scp.h, s_scp.tgt(ii + 1))
                    vs = (h * NTS + i) * HD
                    t.matmul(p_out[0:64, 0:128],
                             v_sb[:, vs:vs + HD],
                             probs[ii % 2][:, 0:128],
                             start=True, stop=False)
                    s_av.inc(t.matmul(p_out[0:64, 0:128],
                                      S_sb[hb:hb + 64, :],
                                      phiq[hb:hb + 64, i * 128:(i + 1) * 128],
                                      start=False, stop=True))
                    t.wait_ge(s_kte.h, s_kte.tgt(ii + 1))
                    s_sac.inc(t.matmul(p_S[0:64, 0:HD],
                                       phikt[:, i * HD:(i + 1) * HD],
                                       v_sb[:, vs:vs + HD],
                                       start=True, stop=True))

        @blk.vector
        def _(v):
            for j, dst in ((0, phiq), (1, phik)):
                if j == 1:
                    pass
                for tb in range(S // 512):
                    g = j * (S // 512) + tb
                    if g == 0:
                        v.wait_ge(s_sc.h, s_sc.tgt(0))
                    v.wait_ge(s_qk.h, s_qk.tgt(g + 1))
                    s_qke.inc(v.tensor_scalar(dst[:, tb * 512:(tb + 1) * 512],
                                              p_qk[:, :], 0.0, 1e-6,
                                              ALU.max, ALU.add))
            for hh in range(2):
                h = 2 * hp + hh
                hb = hh * 64
                v.wait_ge(s_rdb.h, s_rdb.tgt(hh * NTS))
                _chain(v, s_vch, v.tensor_scalar(
                    rd_r[:, :], pos_r[:, :], scale_r[0:1, h:h + 1], 1e-6,
                    ALU.mult, ALU.max))
                s_rd.inc(v.reciprocal(rd_r[:, :], rd_r[:, :]))
                v.wait_ge(s_sac.h, s_sac.tgt(hh * NTS))
                _chain(v, s_vch, v.memset(S_32[hb:hb + 64, :], 0.0))
                _chain(v, s_vch, v.memset(S_sb[hb:hb + 64, :], 0.0))
                s_scp.inc(v.nop())
                for i in range(NTS):
                    ii = hh * NTS + i
                    v.wait_ge(s_sc.h, s_sc.tgt(ii + 1))
                    s_pr.inc(v.scalar_tensor_tensor(
                        probs[ii % 2][:, 0:128], p_sc[ii % 2][:, 0:128], 1.0,
                        mask_sb[:, :], ALU.mult, ALU.mult))
                    v.wait_ge(s_rdb.h, s_rdb.tgt(ii + 1))
                    s_rde.inc(v.tensor_copy(bcd[:, :], p_bc[0:64, 0:128]))
                    v.wait_ge(s_av.h, s_av.tgt(ii + 1))
                    v.wait_ge(s_rde.h, s_rde.tgt(ii + 1))
                    s_ao.inc(v.scalar_tensor_tensor(
                        attn_f[hp][hb:hb + 64, i * 128:(i + 1) * 128],
                        p_out[0:64, 0:128], 1.0, bcd[:, :],
                        ALU.mult, ALU.mult))
                    if i < NTS - 1:
                        v.wait_ge(s_sac.h, s_sac.tgt(ii + 1))
                        _chain(v, s_vch, v.tensor_tensor(
                            S_32[hb:hb + 64, :], S_32[hb:hb + 64, :],
                            p_S[0:64, 0:HD], ALU.add))
                        s_scp.inc(v.tensor_copy(S_sb[hb:hb + 64, :],
                                                S_32[hb:hb + 64, :]))

        @blk.scalar
        def _(sc):
            for hh in range(2):
                for i in range(NTS):
                    ii = hh * NTS + i
                    sc.wait_ge(s_kt.h, s_kt.tgt(ii + 1))
                    sc.wait_ge(s_sac.h, s_sac.tgt(max(ii - NTS + 1, 0)))
                    s_kte.inc(sc.copy(phikt[:, i * HD:(i + 1) * HD],
                                      p_kt[ii % 2][:, 0:64]))
    return


def build_l0_attn_a():
    nc = bass.Bass("TRN2", target_bir_lowering=False, debug=False,
                   dynamic_dma_scratch_size=8192)
    ctx = Ctx(nc)
    xin = nc.dram_tensor("xin", [T, DM], F32, kind="ExternalInput").ap()
    wqkv_d = nc.dram_tensor("wqkv", [DM, 3 * DM], F32, kind="ExternalInput").ap()
    wproj_d = nc.dram_tensor("wproj", [DM, DM], F32, kind="ExternalInput").ap()
    alpha_log = nc.dram_tensor("alpha_log", [1, NH], F32, kind="ExternalInput").ap()
    beta_log = nc.dram_tensor("beta_log", [1, NH], F32, kind="ExternalInput").ap()
    xout = nc.dram_tensor("xout", [T, DM], F32, kind="ExternalOutput").ap()
    lnxbf = nc.dram_tensor("lnxbf", [T, DM], BF16, kind="Internal").ap()
    consts = make_consts(ctx)
    emit_l0_attn_a(ctx, consts, xin, xout, wqkv_d, wproj_d, alpha_log,
                   beta_log, lnxbf)
    return nc


L0B_ROT = 32


def _rope_tables_l0b():
    half = L0B_ROT // 2
    inv = 10000.0 ** (-np.arange(half, dtype=np.float64) / half)
    ang = np.outer(inv, np.arange(S, dtype=np.float64))
    cos, sin = np.cos(ang), np.sin(ang)
    ones = np.ones((32, S))
    zeros = np.zeros((32, S))
    cs64 = np.concatenate([cos, cos, ones], axis=0)
    sn64 = np.concatenate([-sin, sin, zeros], axis=0)
    cs = np.concatenate([cs64, cs64], axis=0).astype(np.float32)
    sn = np.concatenate([sn64, sn64], axis=0).astype(np.float32)
    return cs, sn


def _swap16_perm():
    p = np.zeros((128, 128), np.float32)
    for h in range(2):
        b = h * 64
        for i in range(16):
            p[b + i, b + 16 + i] = 1.0
            p[b + 16 + i, b + i] = 1.0
        for i in range(32, 64):
            p[b + i, b + i] = 1.0
    return p


L0B_TRUNC = None
L0B_LEVEL = 99
L0B_GDBG = 3
L0B_DBG_LNXF = None
L0B_NOCONV = False
L0B_DBGOUT = False
L0B_PAIRDUMP = None
L0B_DUMP_R = 0
L0B_NOOUTPROJ = False
OUTPROJ_NOWO = False
L0B_DUMP_HP = 0


def emit_l0_attn_b(ctx, consts, xin, xout, W, lnxbf_hbm, gscan_d, beta_d,
                   tag="l0b"):
    nc = ctx.nc
    emit_ln(ctx, consts, xin, lnxbf_hbm=lnxbf_hbm, tag=tag + "ln")

    cs_np, sn_np = _rope_tables_l0b()
    d_cs = ctx.const("l0b_cs", _bf(cs_np))
    d_sn = ctx.const("l0b_sn", _bf(sn_np))
    d_swp = ctx.const("l0b_swp", _bf(_swap16_perm()))
    mneg_i = np.where(np.triu(np.ones((128, 128))) > 0, 0.0, -1e9)
    mneg_s = np.where(np.triu(np.ones((128, 128)), 1) > 0, 0.0, -1e9)
    d_mni = ctx.const("l0b_mni", mneg_i.astype(np.float32))
    d_mns = ctx.const("l0b_mns", mneg_s.astype(np.float32))
    d_zr16 = ctx.const("l0b_zr16", np.zeros((16, 128), np.float32))

    with ctx.frame():
        wq_sb = [ctx.sbf(tag + "wq", [128, DM], BF16) for _ in range(NDC)]
        wk_sb = [ctx.sbf(tag + "wk", [128, DM], BF16) for _ in range(NDC)]
        wv_sb = [ctx.sbf(tag + "wv", [128, DM], BF16) for _ in range(NDC)]
        wg_sb = [ctx.sbf(tag + "wg", [128, DM], BF16) for _ in range(NDC)]
        wgkb_sb = [ctx.sbf(tag + "wgkb", [128, 2 * NH], BF16) for _ in range(NDC)]
        cq_sb = [ctx.sbf(tag + "cq", [128, 4], F32) for _ in range(NDC)]
        ck_sb = [ctx.sbf(tag + "ck", [128, 4], F32) for _ in range(NDC)]
        cv_sb = [ctx.sbf(tag + "cv", [128, 4], F32) for _ in range(NDC)]
        cw_row = ctx.sbf(tag + "cwrow", [4, DM], F32)
        cs_sb = ctx.sbf(tag + "cs", [128, S], BF16)
        sn_sb = ctx.sbf(tag + "sn", [128, S], BF16)
        swp_sb = ctx.sbf(tag + "swp", [128, 128], BF16)
        mni_sb = ctx.sbf(tag + "mni", [128, 128], F32)
        mns_sb = ctx.sbf(tag + "mns", [128, 128], F32)
        zr16_sb = ctx.sbf(tag + "zr16", [16, 128], F32)
        alog_r = ctx.sbf(tag + "alog", [1, NH], F32)
        bb_r = ctx.sbf(tag + "bb", [1, NH], F32)
        dtb_r = ctx.sbf(tag + "dtb", [1, NH], F32)
        gnw_r = ctx.sbf(tag + "gnw", [1, HD], F32)
        gnw_bc = ctx.sbf(tag + "gnwb", [128, HD], F32)
        nega_c = ctx.sbf(tag + "nega", [16, 1], F32)
        dtb_c = ctx.sbf(tag + "dtbc", [16, 1], F32)
        bb_c = ctx.sbf(tag + "bbc", [16, 1], F32)

        s_wld = ctx.sem("SEMFAM_wld")
        s_cwl = [ctx.sem(f"SEMFAM_cwl{j}") for j in range(3)]
        s_pre = ctx.sem("SEMFAM_spA")
        s_preb = ctx.sem("SEMFAM_spB")
        s_vch = ctx.sem("SEMFAM_vch")
        s_sch = ctx.sem("SEMFAM_ch")
        p_any = ctx.pf[0]
        with nc.Block() as blk:
            @blk.gpsimd
            def _(g):
                for c in range(NDC):
                    sl = slice(c * 128, (c + 1) * 128)
                    s_wld.inc(g.dma_start(out=wq_sb[c][:], in_=W["wq"][sl, :]), 16)
                    s_wld.inc(g.dma_start(out=wk_sb[c][:], in_=W["wk"][sl, :]), 16)
                    s_wld.inc(g.dma_start(out=wv_sb[c][:], in_=W["wv"][sl, :]), 16)
                    s_wld.inc(g.dma_start(out=wg_sb[c][:], in_=W["wg"][sl, :]), 16)
                    s_wld.inc(g.dma_start(out=wgkb_sb[c][:, 0:NH],
                                          in_=W["wgk"][sl, :]), 16)
                    s_wld.inc(g.dma_start(out=wgkb_sb[c][:, NH:2 * NH],
                                          in_=W["wb"][sl, :]), 16)

                s_wld.inc(g.dma_start(out=cs_sb[:], in_=d_cs[:]), 16)
                s_wld.inc(g.dma_start(out=sn_sb[:], in_=d_sn[:]), 16)
                s_wld.inc(g.dma_start(out=swp_sb[:], in_=d_swp[:]), 16)
                s_wld.inc(g.dma_start(out=mni_sb[:], in_=d_mni[:]), 16)
                s_wld.inc(g.dma_start(out=mns_sb[:], in_=d_mns[:]), 16)
                s_wld.inc(g.dma_start(out=zr16_sb[:], in_=d_zr16[:]), 16)
                s_wld.inc(g.dma_start(out=alog_r[:], in_=W["alog"][:]), 16)
                s_wld.inc(g.dma_start(out=bb_r[:], in_=W["bb"][:]), 16)
                s_wld.inc(g.dma_start(out=dtb_r[:], in_=W["dtb"][:]), 16)
                s_wld.inc(g.dma_start(out=gnw_r[:], in_=W["gnw"][:]), 16)
                for jj, wname in ((0, "cqw"), (1, "ckw"), (2, "cvw")):
                    if jj > 0:
                        g.wait_ge(s_preb.h, s_preb.tgt(jj * NDC))
                    s_cwl[jj].inc(g.dma_start(out=cw_row[:, :],
                                              in_=W[wname][:, :]), 16)

            @blk.tensor
            def _(t):
                t.wait_ge(s_wld.h, s_wld.tgt(s_wld.v - s_wld.b))
                for jj, dst in ((0, cq_sb), (1, ck_sb), (2, cv_sb)):
                    t.wait_ge(s_cwl[jj].h, s_cwl[jj].tgt(16))
                    for c in range(NDC):
                        kk = jj * NDC + c
                        if kk > 0:
                            t.wait_ge(s_preb.h, s_preb.tgt(kk))
                        s_pre.inc(t.transpose(
                            ctx.pf[2][:, 0:4],
                            cw_row[0:4, c * 128:(c + 1) * 128],
                            consts["ident_f32"][0:4, 0:4]))
                NCV = 3 * NDC
                s_pre.inc(t.transpose(ctx.pf[1][0:NH, 0:1], alog_r[0:1, :],
                                      consts["ones_row_f32"][0:1, 0:1]))
                t.wait_ge(s_preb.h, s_preb.tgt(NCV + 1))
                s_pre.inc(t.transpose(ctx.pf[1][0:NH, 0:1], dtb_r[0:1, :],
                                      consts["ones_row_f32"][0:1, 0:1]))
                t.wait_ge(s_preb.h, s_preb.tgt(NCV + 2))
                s_pre.inc(t.transpose(ctx.pf[1][0:NH, 0:1], bb_r[0:1, :],
                                      consts["ones_row_f32"][0:1, 0:1]))
                t.wait_ge(s_preb.h, s_preb.tgt(NCV + 3))
                s_pre.inc(t.matmul(p_any[:, 0:HD], consts["ones_row_f32"][:],
                                   gnw_r[0:1, :], start=True, stop=True))

            @blk.vector
            def _(v):
                kk = 0
                for jj, dst in ((0, cq_sb), (1, ck_sb), (2, cv_sb)):
                    for c in range(NDC):
                        v.wait_ge(s_pre.h, s_pre.tgt(kk + 1))
                        _chain(v, s_vch, v.tensor_copy(dst[c][:],
                                                       ctx.pf[2][:, 0:4]))
                        s_preb.inc(v.nop())
                        kk += 1
                v.wait_ge(s_pre.h, s_pre.tgt(kk + 1))
                _chain(v, s_vch, v.tensor_copy(nega_c[:], ctx.pf[1][0:NH, 0:1]))
                s_preb.inc(v.nop())
                v.wait_ge(s_pre.h, s_pre.tgt(kk + 2))
                _chain(v, s_vch, v.tensor_copy(dtb_c[:], ctx.pf[1][0:NH, 0:1]))
                s_preb.inc(v.nop())
                v.wait_ge(s_pre.h, s_pre.tgt(kk + 3))
                _chain(v, s_vch, v.tensor_copy(bb_c[:], ctx.pf[1][0:NH, 0:1]))
                s_preb.inc(v.nop())
                v.wait_ge(s_pre.h, s_pre.tgt(kk + 4))
                _chain(v, s_vch, v.tensor_copy(gnw_bc[:], p_any[:, 0:HD]))
                s_preb.inc(v.nop())

            @blk.scalar
            def _(sc):
                sc.wait_ge(s_preb.h, s_preb.tgt(3 * NDC + 1))
                _chain(sc, s_sch, sc.activation(nega_c[:], nega_c[:], AF.Exp))
                _chain(sc, s_sch, sc.mul(nega_c[:], nega_c[:], -1.0))
        ctx.barrier()

        for r in range(BL):
            ttag = f"{tag}r{r}"
            with ctx.frame():
                lnxf = [ctx.sbf(ttag + "lnxf", [128, S], BF16)
                        for _ in range(NDC)]
                attn_f = [ctx.sbf(ttag + "af", [128, S], BF16)
                          for _ in range(NDC)]

                emit_tok2feat(ctx, consts, lnxbf_hbm, r * S, S, lnxf, 0,
                              ttag + "tf")
                if L0B_PAIRDUMP is not None and "lnxf1" in L0B_PAIRDUMP \
                        and r == 1:
                    sdm3 = ctx.sem("SEMFAM_dmp3")
                    with nc.Block() as blkE:
                        @blkE.sync
                        def _(s):
                            for c in range(NDC):
                                sdm3.inc(s.dma_start(
                                    out=L0B_PAIRDUMP["lnxf1"][c * 128:(c + 1) * 128, :],
                                    in_=lnxf[c][:]), 16)
                            s.wait_ge(sdm3.h, sdm3.tgt(16 * NDC))
                    ctx.barrier()

                _emit_l0b_gates(ctx, consts, lnxf, wgkb_sb, nega_c, dtb_c,
                                bb_c, zr16_sb, gscan_d, beta_d, ttag)
                if L0B_TRUNC == "gates":
                    continue

                with ctx.frame():
                    npairs = 1 if L0B_TRUNC in ("prep", "chunks1", "pairs1") \
                        else NH // 2
                    for hp in range(npairs):
                        with ctx.frame():
                            _emit_l0b_pair(ctx, consts, r, hp, lnxf,
                                           wq_sb, wk_sb, wv_sb, wg_sb,
                                           cq_sb, ck_sb, cv_sb,
                                           cs_sb, sn_sb, swp_sb, mni_sb,
                                           mns_sb, gnw_bc, gscan_d,
                                           beta_d, attn_f, tag=f"{ttag}p{hp}")
                        ctx.barrier()

                if not L0B_NOOUTPROJ:
                    with ctx.frame():
                        _emit_outproj_residual(ctx, consts, r, xin, xout,
                                               attn_f, W["wo"],
                                               tag=f"{ttag}s4")
                    ctx.barrier()
        if L0B_PAIRDUMP is not None and "negac" in L0B_PAIRDUMP:
            sdm2 = ctx.sem("SEMFAM_dmp2")
            with nc.Block() as blkD:
                @blkD.sync
                def _(s):
                    sdm2.inc(s.dma_start(out=L0B_PAIRDUMP["negac"][:, :],
                                         in_=nega_c[:]), 16)
                    sdm2.inc(s.dma_start(out=L0B_PAIRDUMP["dtbc"][:, :],
                                         in_=dtb_c[:]), 16)
                    sdm2.inc(s.dma_start(out=L0B_PAIRDUMP["wgkb0"][:, :],
                                         in_=wgkb_sb[0][:]), 16)
                    s.wait_ge(sdm2.h, sdm2.tgt(48))
            ctx.barrier()
    return


def _emit_l0b_gates(ctx, consts, lnxf, wgkb_sb, nega_c, dtb_c, bb_c, zr16_sb,
                    gscan_d, beta_d, ttag):
    global L0B_DBG_LNXF
    """gk = -exp(A_log)*softplus(x@Wgk+dt_bias) -> per-chunk inclusive cumsum
    -> gscan_d [16,S]; beta = sigmoid(x@Wb+bb) -> beta_d [16,S].
    wgkb_sb chunks hold [Wgk | Wb] as [128, 32]."""
    nc = ctx.nc
    with ctx.frame():
        gk_t = ctx.sbf(ttag + "gk", [16, S], F32)
        bt_t = ctx.sbf(ttag + "bt", [16, S], F32)
        t1 = ctx.sbf(ttag + "t1", [16, S], F32)
        t2 = ctx.sbf(ttag + "t2", [16, S], F32)
        s_gm = ctx.sem("SEMFAM_gm")
        s_ge = ctx.sem("SEMFAM_ge")
        s_sp2 = ctx.sem("SEMFAM_sp2")
        s_gst = [ctx.sem("SEMFAM_gst0"), ctx.sem("SEMFAM_gst1")]
        s_vch = ctx.sem("SEMFAM_vch")
        s_sch = ctx.sem("SEMFAM_ch")
        p_g = [ctx.pf[2], ctx.pf[3]]

        with nc.Block() as blk:
            @blk.tensor
            def _(t):
                k = 0
                for tb in range(S // 512):
                    for j in range(2):
                        pbk = k % 2
                        t.wait_ge(s_ge.h, s_ge.tgt(max(k - 1, 0)))
                        for c in range(NDC):
                            mm = t.matmul(p_g[pbk][0:NH, :],
                                          wgkb_sb[c][:, j * NH:(j + 1) * NH],
                                          lnxf[c][:, tb * 512:(tb + 1) * 512],
                                          start=(c == 0), stop=(c == NDC - 1))
                        s_gm.inc(mm)
                        k += 1

            @blk.vector
            def _(v):
                k = 0
                for tb in range(S // 512):
                    sl = slice(tb * 512, (tb + 1) * 512)
                    for j in range(2):
                        v.wait_ge(s_gm.h, s_gm.tgt(k + 1))
                        if j == 0:
                            s_ge.inc(v.tensor_scalar(
                                gk_t[:, sl], p_g[k % 2][0:NH, :],
                                dtb_c[0:NH, :], None, ALU.add))
                        else:
                            s_ge.inc(v.tensor_scalar(
                                bt_t[:, sl], p_g[k % 2][0:NH, :],
                                bb_c[0:NH, :], None, ALU.add))
                        k += 1
                v.wait_ge(s_sp2.h, s_sp2.tgt(1))
                if L0B_GDBG >= 2:
                    _chain(v, s_vch, v.tensor_tensor(t1[:], t1[:], t2[:],
                                                     ALU.add))
                    _chain(v, s_vch, v.tensor_scalar(gk_t[:], t1[:],
                                                     nega_c[0:NH, :], None,
                                                     ALU.mult))
                if L0B_GDBG >= 3:
                    for i in range(NTS):
                        csl = slice(i * 128, (i + 1) * 128)
                        _chain(v, s_vch, v.tensor_tensor_scan(
                            gk_t[:, csl], gk_t[:, csl], zr16_sb[:, :], 0.0,
                            ALU.add, ALU.add))
                s_sp2.inc(v.nop())

            @blk.scalar
            def _(sc):
                sc.wait_ge(s_ge.h, s_ge.tgt(8))
                if L0B_GDBG >= 2:
                    _chain(sc, s_sch, sc.activation(t1[:], gk_t[:], AF.Abs))
                    _chain(sc, s_sch, sc.activation(t1[:], t1[:], AF.Exp,
                                                    scale=-1.0))
                    _chain(sc, s_sch, sc.activation(t1[:], t1[:], AF.Ln,
                                                    bias=1.0))
                    _chain(sc, s_sch, sc.activation(t2[:], gk_t[:], AF.Relu))
                    s_sp2.inc(sc.nop())
                    _chain(sc, s_sch, sc.activation(bt_t[:], bt_t[:],
                                                    AF.Sigmoid))
                    s_sp2.inc(sc.nop())
                else:
                    s_sp2.inc(sc.nop())
                    s_sp2.inc(sc.nop())

            @blk.sync
            def _(s):
                s.wait_ge(s_sp2.h, s_sp2.tgt(3))
                s_gst[0].inc(s.dma_start(out=gscan_d[:, :], in_=gk_t[:]), 16)
                s_gst[1].inc(s.dma_start(out=beta_d[:, :], in_=bt_t[:]), 16)

            @blk.gpsimd
            def _(g):
                g.wait_ge(s_gst[0].h, s_gst[0].tgt(16))
                g.wait_ge(s_gst[1].h, s_gst[1].tgt(16))
        ctx.barrier()
    return


class Steps:

    def __init__(self, ctx, dma_engines=("y",)):
        self.ctx = ctx
        self.steps = []
        self.dma_engines = set(dma_engines)

    def add(self, eng, fn, deps=(), dma=False):
        self.steps.append((eng, fn, [d for d in deps if d is not None], dma))
        return len(self.steps) - 1

    def emit(self, nc):
        engs = sorted({e for e, _, _, _ in self.steps})
        sems = {e: self.ctx.sem(f"STEPS_{e}") for e in engs}
        seq = {}
        cnt = {e: 0 for e in engs}
        dmacnt = {e: 0 for e in engs}
        POOL = 8
        dmatot = {}
        dmalast = {}
        for i, (e, fn, deps, dma) in enumerate(self.steps):
            if dma:
                key = f"STEPS_{e}dma{dmacnt[e] % POOL}"
                dmacnt[e] += 1
                if key not in sems:
                    sems[key] = self.ctx.sem(key)
                    dmatot[key] = 0
                if key in dmalast:
                    deps.append(dmalast[key])
                dmalast[key] = i
                dmatot[key] += 16
                seq[i] = (key, dmatot[key])
            else:
                cnt[e] += 1
                seq[i] = (e, cnt[e])

        def mk(eng_name, eng):
            for i, (e, fn, deps, dma) in enumerate(self.steps):
                if e != eng_name:
                    continue
                want = {}
                for d in deps:
                    dk, dseq = seq[d]
                    want[dk] = max(want.get(dk, 0), dseq)
                for dk, dseq in want.items():
                    eng.wait_ge(sems[dk].h, sems[dk].tgt(dseq))
                instr = fn(eng)
                key = seq[i][0]
                sems[key].inc(instr, 16 if dma else 1)

        with nc.Block() as blk:
            for e in engs:
                if e == "t":
                    blk.tensor(lambda eng, e=e: mk(e, eng))
                elif e == "v":
                    blk.vector(lambda eng, e=e: mk(e, eng))
                elif e == "s":
                    blk.scalar(lambda eng, e=e: mk(e, eng))
                elif e == "g":
                    blk.gpsimd(lambda eng, e=e: mk(e, eng))
                elif e == "y":
                    blk.sync(lambda eng, e=e: mk(e, eng))
        return sems


def _emit_l0b_pair(ctx, consts, r, hp, lnxf, wq_sb, wk_sb, wv_sb, wg_sb,
                   cq_sb, ck_sb, cv_sb, cs_sb, sn_sb, swp_sb, mni_sb, mns_sb,
                   gnw_bc, gscan_d, beta_d, attn_f, tag):
    nc = ctx.nc
    NSQ = 7
    NB2 = S // 512

    zb = ctx.sbf(tag + "zb", [128, S + 4], BF16)
    cvo = ctx.sbf(tag + "cvo", [128, S], BF16)
    qf = ctx.sbf(tag + "qf", [128, S], BF16)
    kf = ctx.sbf(tag + "kf", [128, S], BF16)
    sil = ctx.sbf(tag + "sil", [128, S], BF16)
    sqb = ctx.sbf(tag + "sqb", [128, S], BF16)
    nrm = ctx.sbf(tag + "nrm", [1, S], F32)
    rnb = ctx.sbf(tag + "rnb", [128, S], BF16)
    ktok = ctx.sbf(tag + "ktok", [128, 2 * NTS * HD], BF16)
    vtok = ctx.sbf(tag + "vtok", [128, 2 * NTS * HD], BF16)
    gbp = ctx.sbf(tag + "gbp", [128, S], F32)
    grow = gbp[0:1, :]
    brow = gbp[32:33, :]
    nbrow = ctx.sbf(tag + "nb2", [1, S], F32)
    g_sb = ctx.sbf(tag + "gsb", [128, 2 * NTS * HD], BF16)
    gcol = ctx.sbf(tag + "gcol", [128, 1], F32)
    bcol = ctx.sbf(tag + "bcol", [128, 1], F32)
    ngl = ctx.sbf(tag + "ngl", [128, 1], F32)
    glc = ctx.sbf(tag + "glc", [128, 1], F32)
    egl = ctx.sbf(tag + "egl", [128, 1], F32)
    gpos = ctx.sbf(tag + "gpos", [128, 1], F32)
    gneg = ctx.sbf(tag + "gneg", [128, 1], F32)
    dec = ctx.sbf(tag + "dec", [128, 1], F32)
    arg = ctx.sbf(tag + "argb", [128, 128], F32)
    dti = ctx.sbf(tag + "dti", [128, 128], BF16)
    dts = ctx.sbf(tag + "dts", [128, 128], BF16)
    bbc = ctx.sbf(tag + "bbc", [128, 128], BF16)
    Ptmp = ctx.sbf(tag + "Ptmp", [128, 128], BF16)
    P_sb = ctx.sbf(tag + "P", [128, 128], BF16)
    PT_sb = ctx.sbf(tag + "PT", [128, 128], BF16)
    MT_sb = ctx.sbf(tag + "MT", [128, 128], BF16)
    Btmp = ctx.sbf(tag + "Btmp", [128, HD], F32)
    X_sb = ctx.sbf(tag + "X", [128, HD], BF16)
    kbar = ctx.sbf(tag + "kbar", [128, HD], BF16)
    o1 = ctx.sbf(tag + "o1", [128, HD], F32)
    o2 = ctx.sbf(tag + "o2", [128, HD], F32)
    ssq = ctx.sbf(tag + "ssq", [128, 1], F32)
    rst = ctx.sbf(tag + "rst", [128, 1], F32)
    S_32 = ctx.sbf(tag + "S32", [128, HD], F32)
    S3t = ctx.sbf(tag + "S3t", [128, HD], F32)
    S_sb = ctx.sbf(tag + "Ssb", [128, HD], BF16)

    p_a = [ctx.pf[0], ctx.pf[1]]
    p_b = ctx.pf[2]
    p_c = ctx.pf[3]
    p_d = ctx.pf[4]
    p_e = ctx.pf[5]
    p_t = [ctx.pb[0], ctx.pb[1]]

    st = Steps(ctx)
    A = st.add


    zzero = A("v", lambda e: e.memset(zb[:, 0:4], 0.0))
    silu_done = {}
    prev_evac = [None, None]
    prev_group = [None, None]
    for j, wsb in ((0, wq_sb), (1, wk_sb), (2, wv_sb)):
        evs = []
        for tb in range(NB2):
            bk = (j * NB2 + tb) % 2

            def mmf(e, j=j, tb=tb, bk=bk, wsb=wsb):
                for c in range(NDC):
                    mm = e.matmul(p_a[bk][:, :],
                                  wsb[c][:, hp * 128:(hp + 1) * 128],
                                  lnxf[c][:, tb * 512:(tb + 1) * 512],
                                  start=(c == 0), stop=(c == NDC - 1))
                return mm
            mmi = A("t", mmf, deps=[prev_evac[bk]])
            evi = A("v", lambda e, tb=tb, bk=bk: e.tensor_copy(
                zb[:, 4 + tb * 512: 4 + (tb + 1) * 512], p_a[bk][:, :]),
                deps=[mmi, zzero])
            prev_evac[bk] = evi
            evs.append(evi)
        cw = (cq_sb, ck_sb, cv_sb)[j]
        cv0 = A("v", lambda e, cw=cw: e.tensor_scalar(
            cvo[:], zb[:, 1:1 + S], cw[hp][:, 0:1], None, ALU.mult),
            deps=evs)
        for i in range(1, 4):
            cv0 = A("v", lambda e, i=i, cw=cw: e.scalar_tensor_tensor(
                cvo[:], zb[:, 1 + i:1 + i + S], cw[hp][:, i:i + 1], cvo[:],
                ALU.mult, ALU.add), deps=[cv0])
        dstj = (qf, kf, sil)[j]
        sj = A("s", lambda e, dstj=dstj: e.activation(dstj[:], cvo[:], AF.Silu),
               deps=[cv0])
        silu_done[j] = sj
        prev_evac = [sj, sj]

    if L0B_TRUNC == "prep" and L0B_LEVEL < 0.2:
        st.emit(nc)
        return
    rope_done = {}
    prev_l2e = None
    prev_bce = None
    prev_bcm = None
    for j in range(2):
        srcb = (qf, kf)[j]
        sq_deps = [silu_done[j]]
        if j == 1:
            sq_deps.append(rope_done[0])
        sq_s = A("s", lambda e, srcb=srcb: e.activation(sqb[:], srcb[:],
                                                        AF.Square),
                 deps=sq_deps)
        rn_rows = []
        for hh in range(2):
            hb = hh * 64
            for tb in range(NB2):
                mmi = A("t", lambda e, hb=hb, tb=tb: e.matmul(
                    p_b[0:1, :], consts["ones_col_bf"][hb:hb + 64, 0:1],
                    sqb[hb:hb + 64, tb * 512:(tb + 1) * 512],
                    start=True, stop=True), deps=[sq_s, prev_l2e, prev_bce])
                prev_l2e = A("v", lambda e, tb=tb: e.tensor_copy(
                    nrm[0:1, tb * 512:(tb + 1) * 512], p_b[0:1, :]),
                    deps=[mmi])
            r1 = A("s", lambda e: e.activation(nrm[0:1, :], nrm[0:1, :],
                                               AF.Ln, bias=1e-6),
                   deps=[prev_l2e])
            r2 = A("s", lambda e: e.activation(nrm[0:1, :], nrm[0:1, :],
                                               AF.Exp, scale=-0.5), deps=[r1])
            for tb in range(NB2):
                bmi = A("t", lambda e, tb=tb: e.matmul(
                    p_b[0:64, :], consts["ones_row_f32"][0:1, 0:64],
                    nrm[0:1, tb * 512:(tb + 1) * 512], start=True, stop=True),
                    deps=[r2, prev_bce])
                prev_bcm = bmi
                prev_bce = A("v", lambda e, hb=hb, tb=tb: e.tensor_copy(
                    rnb[hb:hb + 64, tb * 512:(tb + 1) * 512], p_b[0:64, :]),
                    deps=[bmi])
            rn_rows.append(prev_bce)
        qn = A("v", lambda e, srcb=srcb: e.tensor_tensor(
            srcb[:], srcb[:], rnb[:], ALU.mult), deps=rn_rows + [sq_s])
        mul_cs = A("v", lambda e, srcb=srcb: e.tensor_tensor(
            sqb[:], srcb[:], cs_sb[:], ALU.mult), deps=[qn])
        last = None
        for tb in range(NB2):
            smi = A("t", lambda e, srcb=srcb, tb=tb: e.matmul(
                p_a[tb % 2][:, 0:512], swp_sb[:],
                srcb[:, tb * 512:(tb + 1) * 512], start=True, stop=True),
                deps=[qn, last])
            swe = A("v", lambda e, tb=tb: e.scalar_tensor_tensor(
                rnb[:, tb * 512:(tb + 1) * 512], p_a[tb % 2][:, 0:512], 1.0,
                sn_sb[:, tb * 512:(tb + 1) * 512], ALU.mult, ALU.mult),
                deps=[smi, mul_cs])
            last = A("v", lambda e, srcb=srcb, tb=tb: e.tensor_tensor(
                srcb[:, tb * 512:(tb + 1) * 512],
                sqb[:, tb * 512:(tb + 1) * 512],
                rnb[:, tb * 512:(tb + 1) * 512], ALU.add), deps=[swe])
        rope_done[j] = last

    if L0B_TRUNC == "prep" and L0B_LEVEL < 0.4:
        st.emit(nc)
        return
    tok_done = {}
    prev_t = [None, None]
    for src_i, (srcb, dstb) in ((0, (kf, ktok)), (1, (sil, vtok))):
        dep0 = rope_done[1] if src_i == 0 else silu_done[2]
        evs = []
        kk = 0
        for hh in range(2):
            hb = hh * 64
            for i in range(NTS):
                bk = kk % 2
                tmi = A("t", lambda e, hb=hb, i=i, bk=bk, srcb=srcb: e.transpose(
                    p_t[bk][:, 0:64], srcb[hb:hb + 64, i * 128:(i + 1) * 128],
                    consts["ident_bf"][hb:hb + 64, hb:hb + 64]),
                    deps=[dep0, prev_t[bk]])
                tei = A("v", lambda e, hh=hh, i=i, bk=bk, dstb=dstb:
                        e.tensor_copy(
                            dstb[:, (hh * NTS + i) * HD:(hh * NTS + i + 1) * HD],
                            p_t[bk][:, 0:64]), deps=[tmi])
                prev_t[bk] = tei
                evs.append(tei)
                kk += 1
        tok_done[src_i] = evs

    if L0B_TRUNC == "prep" and L0B_LEVEL < 0.6:
        st.emit(nc)
        return
    g_done = []
    prev_g = [None, None]
    for i in range(NTS):
        bk = i % 2

        def gmf(e, i=i, bk=bk):
            for c in range(NDC):
                mm = e.matmul(p_a[bk][:, 0:128],
                              lnxf[c][:, i * 128:(i + 1) * 128],
                              wg_sb[c][:, hp * 128:(hp + 1) * 128],
                              start=(c == 0), stop=(c == NDC - 1))
            return mm
        gmi = A("t", gmf, deps=[prev_g[bk], rope_done[1]])
        ge0 = A("s", lambda e, i=i, bk=bk: e.activation(
            g_sb[:, (0 * NTS + i) * HD:(0 * NTS + i + 1) * HD],
            p_a[bk][:, 0:64], AF.Silu), deps=[gmi])
        ge1 = A("s", lambda e, i=i, bk=bk: e.activation(
            g_sb[:, (1 * NTS + i) * HD:(1 * NTS + i + 1) * HD],
            p_a[bk][:, 64:128], AF.Silu), deps=[gmi, ge0])
        prev_g[bk] = ge1
        g_done.append(ge1)

    if L0B_TRUNC == "prep":
        st.emit(nc)
        return
    sinit = A("v", lambda e: e.memset(S_32[:], 0.0),
              deps=[rope_done[0], rope_done[1], prev_bce, prev_l2e,
                    tok_done[0][-1], tok_done[1][-1], g_done[-1]])
    sinit2 = A("v", lambda e: e.memset(S_sb[:], 0.0), deps=[sinit])
    prev_chunk_end = sinit2
    prev_attn = None
    nheads = 1 if L0B_TRUNC == "chunks1" else 2
    for hh in range(nheads):
        h = 2 * hp + hh
        hb = hh * 64
        if hh == 1:
            sinit = A("v", lambda e: e.memset(S_32[:], 0.0),
                      deps=[prev_chunk_end])
            sinit2 = A("v", lambda e: e.memset(S_sb[:], 0.0), deps=[sinit])
            prev_chunk_end = sinit2
        ld_g = A("y", lambda e, h=h: e.dma_start(out=grow,
                                                 in_=gscan_d[h:h + 1, :]),
                 dma=True, deps=[prev_chunk_end])
        ld_b = A("y", lambda e, h=h: e.dma_start(out=brow,
                                                 in_=beta_d[h:h + 1, :]),
                 dma=True, deps=[prev_chunk_end])
        nb_st = A("v", lambda e: e.tensor_scalar_mul(nbrow[:], brow, -1.0),
                  deps=[ld_b])
        nchunks = 2 if L0B_TRUNC == "chunks1" else NTS
        for i in range(nchunks):
            csl = slice(i * 128, (i + 1) * 128)
            vs = (hh * NTS + i) * HD
            pe = prev_chunk_end
            m1 = A("t", lambda e, csl=csl: e.transpose(
                p_d[:, 0:1], grow[:, csl],
                consts["ones_row_f32"][0:1, 0:1]), deps=[ld_g, pe])
            m2 = A("t", lambda e, i=i: e.matmul(
                p_d[:, 1:2], consts["ones_row_f32"][:],
                grow[:, i * 128 + 127:i * 128 + 128], start=True, stop=True),
                deps=[ld_g, pe])
            m3 = A("t", lambda e, csl=csl: e.transpose(
                p_d[:, 2:3], brow[0:1, csl],
                consts["ones_f32_fh"][32:33, 0:1]), deps=[ld_b, pe])
            e1 = A("v", lambda e: e.tensor_copy(gcol[:], p_d[:, 0:1]),
                   deps=[m1, m2, m3])
            e2 = A("v", lambda e: e.tensor_copy(glc[:], p_d[:, 1:2]),
                   deps=[m2, m3])
            e3 = A("v", lambda e: e.tensor_copy(bcol[:], p_d[:, 2:3]),
                   deps=[m3])
            c1 = A("s", lambda e: e.activation(gpos[:], gcol[:], AF.Exp),
                   deps=[e1])
            c2 = A("s", lambda e: e.mul(gneg[:], gpos[:], -1.0), deps=[c1])
            c3 = A("s", lambda e: e.mul(ngl[:], glc[:], -1.0), deps=[e2])
            c4 = A("s", lambda e: e.activation(egl[:], glc[:], AF.Exp),
                   deps=[e2])
            c5 = A("s", lambda e: e.activation(dec[:], gcol[:], AF.Exp,
                                               scale=-1.0, bias=glc[:]),
                   deps=[e2, e1])
            if L0B_LEVEL < 2:
                prev_chunk_end = e3
                prev_attn = None if prev_attn is None else prev_attn
                continue
            m4 = A("t", lambda e, csl=csl: e.matmul(
                p_b[:, 0:128], consts["ones_row_f32"][:], grow[:, csl],
                start=True, stop=True), deps=[ld_g, pe])
            m5 = A("t", lambda e, csl=csl: e.matmul(
                p_b[:, 128:256], consts["ones_row_f32"][:],
                nbrow[:, csl], start=True, stop=True), deps=[nb_st, pe])
            bb1 = A("v", lambda e: e.tensor_copy(bbc[:], p_b[:, 128:256]),
                    deps=[m5, m4])
            if L0B_LEVEL < 2.3:
                prev_chunk_end = bb1
                continue
            a1 = A("v", lambda e: e.scalar_tensor_tensor(
                arg[:], p_b[:, 0:128], gcol[:], mni_sb[:],
                ALU.subtract, ALU.add), deps=[m4, m5, e1, bb1])
            d1 = A("s", lambda e: e.activation(dti[:], arg[:], AF.Exp),
                   deps=[a1])
            if L0B_LEVEL < 2.6:
                prev_chunk_end = d1
                continue
            a2 = A("v", lambda e: e.scalar_tensor_tensor(
                arg[:], p_b[:, 0:128], gcol[:], mns_sb[:],
                ALU.subtract, ALU.add), deps=[m4, m5, e1, d1])
            d2 = A("s", lambda e: e.activation(dts[:], arg[:], AF.Exp),
                   deps=[a2])
            if L0B_LEVEL < 3:
                prev_chunk_end = bb1
                continue
            m6 = A("t", lambda e, csl=csl, hb=hb: e.matmul(
                p_a[0][:, 0:128], kf[hb:hb + 64, csl], kf[hb:hb + 64, csl],
                start=True, stop=True), deps=[rope_done[1], pe])
            m7 = A("t", lambda e, csl=csl, hb=hb: e.matmul(
                p_a[1][:, 0:128], kf[hb:hb + 64, csl], qf[hb:hb + 64, csl],
                start=True, stop=True), deps=[rope_done[0], rope_done[1], pe])
            p1 = A("v", lambda e: e.scalar_tensor_tensor(
                Ptmp[:], p_a[0][:, 0:128], 1.0, dts[:], ALU.mult, ALU.mult),
                deps=[m6, d2])
            p2 = A("v", lambda e: e.tensor_tensor(P_sb[:], Ptmp[:], bbc[:],
                                                  ALU.mult), deps=[p1, bb1])
            mm1 = A("v", lambda e: e.scalar_tensor_tensor(
                MT_sb[:], p_a[1][:, 0:128], 1.0, dti[:], ALU.mult, ALU.mult),
                deps=[m7, d1])
            if L0B_LEVEL < 4:
                prev_chunk_end = mm1
                continue
            m8 = A("t", lambda e, csl=csl, hb=hb: e.matmul(
                p_c[:, 0:HD], kf[hb:hb + 64, csl], S_sb[hb:hb + 64, :],
                start=True, stop=True), deps=[rope_done[1], pe])
            b1 = A("v", lambda e, vs=vs: e.scalar_tensor_tensor(
                Btmp[:], p_c[:, 0:HD], gneg[:], vtok[:, vs:vs + HD],
                ALU.mult, ALU.add),
                deps=[m8, c2, tok_done[1][hh * NTS + i]])
            b2 = A("v", lambda e: e.tensor_scalar(
                X_sb[:], Btmp[:], bcol[:], None, ALU.mult), deps=[b1, e3])
            if L0B_LEVEL < 5:
                prev_chunk_end = b2
                continue
            xdep = b2
            pdep = p2
            for sq_i in range(NSQ):
                am = A("t", lambda e: (
                    e.matmul(p_c[:, 0:HD], P_sb[:, :], X_sb[:, :],
                             start=True, stop=False),
                    e.matmul(p_c[:, 0:HD], consts["ident_bf"][:], X_sb[:, :],
                             start=False, stop=True))[1],
                    deps=[xdep, pdep])
                xdep = A("v", lambda e: e.tensor_copy(X_sb[:], p_c[:, 0:HD]),
                         deps=[am])
                if sq_i < NSQ - 1:
                    tm = A("t", lambda e: e.transpose(
                        p_t[0][:, 0:128], P_sb[:, :], consts["ident_bf"][:]),
                        deps=[pdep, am])
                    te = A("v", lambda e: e.tensor_copy(PT_sb[:],
                                                        p_t[0][:, 0:128]),
                           deps=[tm])
                    qm = A("t", lambda e: e.matmul(
                        p_a[0][:, 0:128], PT_sb[:, :], P_sb[:, :],
                        start=True, stop=True), deps=[te, p1])
                    pdep = A("v", lambda e: e.tensor_copy(P_sb[:],
                                                          p_a[0][:, 0:128]),
                            deps=[qm])
            if L0B_LEVEL < 6:
                prev_chunk_end = xdep
                continue
            m9 = A("t", lambda e: e.matmul(
                p_c[:, 0:HD], MT_sb[:, :], X_sb[:, :], start=True, stop=True),
                deps=[mm1, xdep])
            o1e = A("v", lambda e: e.tensor_copy(o1[:], p_c[:, 0:HD]),
                    deps=[m9])
            m10 = A("t", lambda e, csl=csl, hb=hb: e.matmul(
                p_b[:, 0:HD], qf[hb:hb + 64, csl], S_sb[hb:hb + 64, :],
                start=True, stop=True),
                deps=[rope_done[0], pe, m5, a1, a2, bb1])
            o2e = A("v", lambda e: e.scalar_tensor_tensor(
                o2[:], p_b[:, 0:HD], gpos[:], o1[:], ALU.mult, ALU.add),
                deps=[m10, o1e, c1])
            g1 = A("s", lambda e: e.activation(o1[:], o2[:], AF.Square,
                                               accum_out=ssq[:]), deps=[o2e])
            g2 = A("s", lambda e: e.activation(rst[:], ssq[:], AF.Ln,
                                               scale=1.0 / HD, bias=1e-5),
                   deps=[g1])
            g3 = A("s", lambda e: e.activation(rst[:], rst[:], AF.Exp,
                                               scale=-0.5), deps=[g2])
            g4 = A("v", lambda e: e.tensor_scalar(
                o2[:], o2[:], rst[:], None, ALU.mult), deps=[g3, g1])
            g5 = A("v", lambda e: e.tensor_tensor(o2[:], o2[:], gnw_bc[:],
                                                  ALU.mult), deps=[g4])
            g6 = A("v", lambda e, vs=vs: e.tensor_tensor(
                o2[:], o2[:], g_sb[:, vs:vs + HD], ALU.mult),
                deps=[g5, g_done[i]])
            g7 = A("v", lambda e: e.tensor_copy(kbar[:], o2[:]), deps=[g6])
            m11 = A("t", lambda e: e.transpose(
                p_t[1][0:64, 0:128], kbar[:, 0:HD], consts["ident_bf"][:]),
                deps=[g7, prev_attn])
            at1 = A("v", lambda e, hb=hb, csl=csl: e.tensor_copy(
                attn_f[hp][hb:hb + 64, csl], p_t[1][0:64, 0:128]),
                deps=[m11])
            if L0B_LEVEL < 7:
                prev_chunk_end = at1
                prev_attn = at1
                continue
            k1 = A("v", lambda e, vs=vs: e.tensor_scalar(
                kbar[:], ktok[:, vs:vs + HD], dec[:], None, ALU.mult),
                deps=[c5, m11, tok_done[0][hh * NTS + i]])
            m12 = A("t", lambda e: e.matmul(
                p_e[0:64, 0:HD], kbar[:, :], X_sb[:, :], start=True,
                stop=True), deps=[k1, xdep, m9])
            s1 = A("v", lambda e, hb=hb: e.tensor_scalar(
                S3t[hb:hb + 64, :], S_32[hb:hb + 64, :], egl[hb:hb + 64, :],
                None, ALU.mult), deps=[c4, m8, m10])
            s2 = A("v", lambda e, hb=hb: e.scalar_tensor_tensor(
                S_32[hb:hb + 64, :], p_e[0:64, 0:HD], 1.0, S3t[hb:hb + 64, :],
                ALU.mult, ALU.add), deps=[m12, s1])
            s3 = A("v", lambda e, hb=hb: e.tensor_copy(
                S_sb[hb:hb + 64, :], S_32[hb:hb + 64, :]), deps=[s2])
            prev_chunk_end = s3
            prev_attn = at1

    st.emit(nc)
    if L0B_PAIRDUMP is not None and hp == L0B_DUMP_HP and r == L0B_DUMP_R:
        ctx.barrier()
        sdmp = ctx.sem("SEMFAM_dmp")
        with nc.Block() as blk2:
            @blk2.sync
            def _(s):
                for nm, tile in (("qf", qf), ("kf", kf), ("sil", sil),
                                 ("ktok", ktok), ("vtok", vtok),
                                 ("gsb", g_sb), ("S32", S_32),
                                 ("attnf0", attn_f[0])):
                    sdmp.inc(s.dma_start(out=L0B_PAIRDUMP[nm][:, :],
                                         in_=tile[:]), 16)
                s.wait_ge(sdmp.h, sdmp.tgt(8 * 16))
        ctx.barrier()
    return


def build_l0_attn_b():
    nc = bass.Bass("TRN2", target_bir_lowering=False, debug=False,
                   dynamic_dma_scratch_size=8192)
    ctx = Ctx(nc)
    xin = nc.dram_tensor("xin", [T, DM], F32, kind="ExternalInput").ap()
    W = {}
    for nm, shape in (("wq", [DM, DM]), ("wk", [DM, DM]), ("wv", [DM, DM]),
                      ("wg", [DM, DM]), ("wo", [DM, DM]),
                      ("wgk", [DM, NH]), ("wb", [DM, NH]),
                      ("cqw", [4, DM]), ("ckw", [4, DM]), ("cvw", [4, DM]),
                      ("gnw", [1, HD]), ("alog", [1, NH]), ("bb", [1, NH]),
                      ("dtb", [1, NH])):
        W[nm] = nc.dram_tensor(nm, shape, F32, kind="ExternalInput").ap()
    xout = nc.dram_tensor("xout", [T, DM], F32, kind="ExternalOutput").ap()
    lnxbf = nc.dram_tensor("lnxbf", [T, DM], BF16, kind="Internal").ap()
    dbg = "ExternalOutput" if L0B_DBGOUT else "Internal"
    gscan_d = nc.dram_tensor("gscan", [NH, S], F32, kind=dbg).ap()
    beta_d = nc.dram_tensor("betad", [NH, S], F32, kind=dbg).ap()
    global L0B_PAIRDUMP
    if L0B_DBGOUT:
        L0B_PAIRDUMP = {}
        for nm, shape in (("lnxf1", [DM, S]), ("negac", [16, 1]),
                          ("dtbc", [16, 1]),
                          ("wgkb0", [128, 2 * NH]),
                          ("qf", [128, S]), ("kf", [128, S]),
                          ("sil", [128, S]), ("ktok", [128, 2 * NTS * HD]),
                          ("vtok", [128, 2 * NTS * HD]),
                          ("gsb", [128, 2 * NTS * HD]), ("S32", [128, HD]),
                          ("attnf0", [128, S])):
            dt = F32 if nm in ("S32", "negac", "dtbc") else BF16
            L0B_PAIRDUMP[nm] = nc.dram_tensor("dmp_" + nm, shape, dt,
                                              kind="ExternalOutput").ap()
    consts = make_consts(ctx)
    emit_l0_attn_b(ctx, consts, xin, xout, W, lnxbf, gscan_d, beta_d)
    return nc



EXP = 8
CAP = 5120
NSLOT = EXP * CAP
NRC2 = 42
SECOUT = CAP + 128
GOUT = EXP * SECOUT
SECROWS = 128 * NRC2
RECROWS = EXP * SECROWS
MOE_I = 4096
NSG = 10
TALL = NCORES * T
GYPAD = T + 128
GYROWS = NCORES * GYPAD
REPL = [list(range(NCORES))]


def _moe_routing(ctx, consts, lv, P, D, tokinfo, slotinfo, recsb, pofr, tag):
    nc = ctx.nc
    topk = 2 if lv == 0 else 1
    st = Steps(ctx)
    A = st.add

    xt = [ctx.sbf(tag + "x", [128, DM], F32) for _ in range(2)]
    ft = [ctx.sbf(tag + "f", [128, 128], F32) for _ in range(2)]
    wrt = [ctx.sbf(tag + "wr", [128, EXP], F32) for _ in range(NDC)]
    brt = ctx.sbf(tag + "br", [1, EXP], F32)
    lgT = ctx.sbf(tag + "lgT", [8, 128], F32)
    lg = ctx.sbf(tag + "lg", [128, EXP], F32)
    mxr = ctx.sbf(tag + "mx", [128, 8], F32)
    mxi = ctx.sbf(tag + "mxi", [128, 8], U32)
    mxif = ctx.sbf(tag + "mxif", [128, 8], F32)
    nmx = ctx.sbf(tag + "nmx", [128, 1], F32)
    smr = ctx.sbf(tag + "smr", [128, 1], F32)
    oh = [ctx.sbf(tag + f"oh{j}", [128, EXP], F32) for j in range(topk)]
    mti = ctx.sbf(tag + "mti", [128, EXP], F32)
    mrow = ctx.sbf(tag + "mrow", [EXP, T], F32)
    crow = ctx.sbf(tag + "crow", [EXP, T], F32)
    zr8 = ctx.sbf(tag + "zr8", [EXP, T], F32)
    cm16r = ctx.sbf(tag + "cmr", [1, 16], F32)
    cm16 = ctx.sbf(tag + "cm", [16, 1], F32)
    cntb = ctx.sbf(tag + "cntb", [16, EXP], F32)
    cnt2 = ctx.sbf(tag + "cnt2", [2, EXP], F32)
    priorg = ctx.sbf(tag + "prg", [EXP, 2], F32)
    prior = ctx.sbf(tag + "prior", [EXP, 1], F32)
    goffr = ctx.sbf(tag + "gofr", [1, EXP], F32)
    goff_bc = ctx.sbf(tag + "gofb", [128, EXP], F32)
    tokid0 = ctx.sbf(tag + "tk0", [128, 1], F32)
    iot8 = ctx.sbf(tag + "io8", [128, EXP], F32)
    dropc = ctx.sbf(tag + "drc", [128, 2], F32)
    zrec = ctx.sbf(tag + "zrec", [128, RECROWS * 8 // 128], F32)
    grT = [ctx.sbf(tag + f"grT{b}", [128, EXP], F32) for b in range(2)]
    grk2 = [ctx.sbf(tag + f"gk2{b}", [128, EXP], F32) for b in range(2)]
    t8 = ctx.sbf(tag + "t8", [128, EXP], F32)
    rkc = ctx.sbf(tag + "rkc", [128, 2], F32)
    slc = ctx.sbf(tag + "slc", [128, 2], F32)
    vlc = ctx.sbf(tag + "vlc", [128, 2], F32)
    ta = ctx.sbf(tag + "ta", [128, 2], F32)
    tb_ = ctx.sbf(tag + "tb", [128, 2], F32)
    tc_ = ctx.sbf(tag + "tc", [128, 2], F32)
    rowf = ctx.sbf(tag + "rwf", [128, 2], F32)
    tg_ = ctx.sbf(tag + "tg", [128, 2], F32)
    tu_ = ctx.sbf(tag + "tu", [128, 2], U32)
    th_ = ctx.sbf(tag + "th", [128, 2], F32)
    ti2_ = ctx.sbf(tag + "ti2", [128, 2], F32)
    scat = [ctx.sbf(tag + f"sct{b}", [128, 8], F32) for b in range(2)]
    scix = [ctx.sbf(tag + f"six{b}", [128, 2], U32) for b in range(2)]
    xbf = [ctx.sbf(tag + f"xb{b}", [128, DM], BF16) for b in range(2)]
    xsix = [ctx.sbf(tag + f"xsx{b}", [128, 2], U32) for b in range(2)]
    zslot = ctx.sbf(tag + "zsl", [128, 16384], BF16)
    tr8 = ctx.sbf(tag + "tr8", [1, EXP], F32)
    tru8 = ctx.sbf(tag + "tru8", [1, EXP], U32)
    trr = ctx.sbf(tag + "trr", [1, EXP], F32)
    tra = ctx.sbf(tag + "tra", [1, EXP], F32)
    tr8b = ctx.sbf(tag + "tr8b", [1, EXP], F32)
    zrow = ctx.sbf(tag + "zrw", [1, EXP], F32)
    gcol = ctx.sbf(tag + "gcl", [EXP, 1], F32)

    d_iot8 = ctx.const("moe_iot8",
                       np.tile(np.arange(EXP, dtype=np.float32), (128, 1)))
    d_tkl = ctx.const("moe_tkl",
                      np.arange(128, dtype=np.float32).reshape(128, 1))
    d_drop = ctx.const("moe_drop",
                       (np.arange(128, dtype=np.float32)[:, None] * NRC2
                        + np.array([40.0, 41.0], np.float32)[None, :]))
    d_goff0 = ctx.const("moe_goff0",
                        (np.arange(EXP, dtype=np.float32) * CAP)
                        .reshape(1, EXP))
    d_dropg = ctx.const("moe_dropg",
                        (CAP + np.arange(128, dtype=np.float32))
                        .reshape(1, 128))
    dropg = ctx.sbf(tag + "drg", [128, 1], F32)
    dropgr = ctx.sbf(tag + "drgr", [1, 128], F32)
    cntr = ctx.sbf(tag + "cnr", [1, EXP], F32)

    pA, pB = ctx.pf[0], ctx.pf[1]
    pL = ctx.pf[2]
    pC = ctx.pf[3]
    pM = ctx.pf[4]
    pP = ctx.pf[5]

    zm2 = A("v", lambda e: e.memset(zslot[:], 0.0))
    zxs = []
    slview = D["slotin"].ap().rearrange("(a b) d -> a (b d)", a=128)
    per = slview.shape[1]
    off = 0
    while off < per and not MOE_NOZX:
        w = min(16384, per - off)
        zxs.append(A("y", lambda e, off=off, w=w: e.dma_start(
            out=slview[:, off:off + w], in_=zslot[:, 0:w]),
            deps=[zm2, zxs[-1] if zxs else None], dma=True))
        off += w
    zm = A("v", lambda e: e.memset(zrec[:], 0.0))
    zs = A("y", lambda e: e.dma_start(
        out=D["rec"].ap().rearrange("(a b) w -> a (b w)", a=128)[:, :],
        in_=zrec[:]), deps=[zm], dma=True)
    lc = []
    for dst, src in ((iot8, d_iot8), (tokid0, P["tokid0"].ap()),
                     (dropc, d_drop), (brt, P["brt"].ap()),
                     (cm16r, P["cmask16"].ap()), (dropgr, d_dropg)):
        lc.append(A("y", lambda e, dst=dst, src=src: e.dma_start(
            out=dst[:], in_=src[:]), dma=True))
    for c in range(NDC):
        lc.append(A("y", lambda e, c=c: e.dma_start(
            out=wrt[c][:], in_=P["wrt"].ap()[c * 128:(c + 1) * 128, :]),
            dma=True))
    zr = A("v", lambda e: e.memset(zr8[:], 0.0))
    zrw = A("v", lambda e: e.memset(zrow[:], 0.0))
    tcm = A("t", lambda e: e.transpose(
        pC[0:16, 0:1], cm16r[:], consts["ident_f32"][0:1, 0:1]),
        deps=[lc[4]])
    ecm = A("v", lambda e: e.tensor_copy(cm16[:], pC[0:16, 0:1]),
            deps=[tcm])
    tdg = A("t", lambda e: e.transpose(
        pM[:, 0:1], dropgr[:], consts["ident_f32"][0:1, 0:1]),
        deps=[lc[5]])
    edg = A("v", lambda e: e.tensor_copy(dropg[:], pM[:, 0:1]),
            deps=[tdg])

    prev_ld = [None, None]
    prev_ev = [None, None]
    prev_lgT = None
    prev_lg = None
    prev_oh = [None] * topk
    prev_mti = None
    prev_sm = None
    prev_t5 = None
    mrow_done = []
    for i in range(NT):
        b = i % 2
        ldi = A("y", lambda e, i=i, b=b: e.dma_start(
            out=xt[b][:], in_=D["lnx32"].ap()[i * 128:(i + 1) * 128, :]),
            deps=[prev_ld[b]], dma=True)
        mm_last = None
        tr_done = []
        for c in range(NDC):
            fb = c % 2
            pT = pA if fb == 0 else pB
            tri = A("t", lambda e, b=b, c=c, pT=pT: e.transpose(
                pT[:, 0:128], xt[b][:, c * 128:(c + 1) * 128],
                consts["ident_f32"][:]), deps=[ldi, prev_ev[fb]])
            evi = A("v", lambda e, fb=fb, pT=pT: e.tensor_copy(
                ft[fb][:], pT[:, 0:128]), deps=[tri])
            prev_ev[fb] = evi
            tr_done.append(tri)
            mm_last = A("t", lambda e, c=c, fb=fb: e.matmul(
                pL[0:8, 0:128], wrt[c][:], ft[fb][:],
                start=(c == 0), stop=False),
                deps=[evi, lc[6 + c]] + ([prev_lgT] if c == 0 else []))
        mm_b = A("t", lambda e: e.matmul(
            pL[0:8, 0:128], brt[0:1, :], consts["ones_row_f32"][0:1, :],
            start=False, stop=True), deps=[mm_last, lc[3]])
        ev_l = A("v", lambda e: e.tensor_copy(lgT[:], pL[0:8, 0:128]),
                 deps=[mm_b, prev_lgT])
        if MOE_DBG5 and i == 0:
            A("y", lambda e: e.dma_start(
                out=MOE_DBG5_T["xt0"].ap()[:, :], in_=xt[0][:]),
                deps=[ldi], dma=True)
            d5a = A("y", lambda e: e.dma_start(
                out=MOE_DBG5_T["lgT"].ap()[:, :], in_=lgT[:]),
                deps=[ev_l], dma=True)
        prev_lgT = ev_l
        tr_l = A("t", lambda e: e.transpose(
            pC[:, 0:8], lgT[:], consts["ident_f32"][0:8, 0:8]),
            deps=[ev_l, prev_lg] + ([ecm] if i == 0 else []))
        ev_l2 = A("v", lambda e: e.tensor_copy(lg[:], pC[:, 0:8]),
                  deps=[tr_l, prev_sm])
        if MOE_DBG5 and i == 0:
            d5b = A("y", lambda e: e.dma_start(
                out=MOE_DBG5_T["lg0"].ap()[:, :], in_=lg[:]),
                deps=[ev_l2], dma=True)
        s1 = A("v", lambda e: e.reduce_max(nmx[:], lg[:], axis=AX.X,
                                           negate=True),
               deps=[ev_l2, prev_sm])
        s2 = A("s", lambda e: e.activation(lg[:], lg[:], AF.Exp,
                                           bias=nmx[:]), deps=[s1])
        s3 = A("v", lambda e: e.reduce_sum(smr[:], lg[:], axis=AX.X),
               deps=[s2])
        s4 = A("v", lambda e: e.reciprocal(smr[:], smr[:]), deps=[s3])
        s5 = A("v", lambda e: e.tensor_scalar(lg[:], lg[:], smr[:], None,
                                              ALU.mult), deps=[s4])
        prev_lg = s5
        t1 = A("v", lambda e: e.max(mxr[:], lg[:]),
               deps=[s5, prev_t5])
        t2 = A("v", lambda e: e.max_index(mxi[:], mxr[:], lg[:]),
               deps=[t1])
        t3 = A("v", lambda e: e.tensor_copy(mxif[:], mxi[:]), deps=[t2])
        if MOE_DBG5 and i == 0:
            d5c = A("y", lambda e: e.dma_start(
                out=MOE_DBG5_T["lg1"].ap()[:, :], in_=lg[:]),
                deps=[s5], dma=True)
            d5d = A("y", lambda e: e.dma_start(
                out=MOE_DBG5_T["mxr"].ap()[:, :], in_=mxr[:]),
                deps=[t3], dma=True)
            d5e = A("y", lambda e: e.dma_start(
                out=MOE_DBG5_T["mxif"].ap()[:, :], in_=mxif[:]),
                deps=[t3], dma=True)
            for dd in (d5a, d5b, d5c, d5d, d5e):
                pass
        prev_sm = t3
        ti = tokinfo[:, i * 4:(i + 1) * 4]
        t4 = A("v", lambda e, ti=ti: e.tensor_copy(ti[:, 0:2], mxif[:, 0:2]),
               deps=[t3])
        t5 = A("v", lambda e, ti=ti: e.tensor_copy(ti[:, 2:4], mxr[:, 0:2]),
               deps=[t2, t4])
        prev_t5 = t5
        ohd = []
        for j in range(topk):
            ohd.append(A("v", lambda e, j=j: e.tensor_scalar(
                oh[j][:], iot8[:], mxif[:, j:j + 1], None, ALU.is_equal),
                deps=[t3, lc[0], prev_oh[j]]))
        if topk == 2:
            mjoin = A("v", lambda e: e.tensor_tensor(
                mti[:], oh[0][:], oh[1][:], ALU.add),
                deps=ohd + [prev_mti])
        else:
            mjoin = A("v", lambda e: e.tensor_copy(mti[:], oh[0][:]),
                      deps=ohd + [prev_mti])
        for j in range(topk):
            prev_oh[j] = mjoin
        tr_m = A("t", lambda e: e.transpose(
            pM[0:8, 0:128], mti[:], consts["ident_f32"][:]),
            deps=[mjoin, prev_mti] + ([edg] if i == 0 else []))
        ev_m = A("v", lambda e, i=i: e.tensor_copy(
            mrow[:, i * 128:(i + 1) * 128], pM[0:8, 0:128]), deps=[tr_m])
        prev_mti = ev_m
        mrow_done.append(ev_m)
        prev_ld[b] = tr_done[-1]

    sc1 = A("v", lambda e: e.tensor_tensor_scan(
        crow[:], mrow[:], zr8[:], 0.0, ALU.add, ALU.add),
        deps=mrow_done + [zr])
    tcr = A("t", lambda e: e.transpose(
        pL[0:1, 0:EXP], crow[:, T - 1:T],
        consts["ident_f32"][0:EXP, 0:EXP]), deps=[sc1])
    cre = A("v", lambda e: e.tensor_copy(cntr[:], pL[0:1, 0:EXP]),
            deps=[tcr])
    bc1 = A("t", lambda e: e.matmul(
        pP[0:16, 0:EXP], consts["ones_row_f32"][0:1, 0:16], cntr[0:1, :],
        start=True, stop=True), deps=[cre])
    bce = A("v", lambda e: e.tensor_copy(cntb[:], pP[0:16, 0:EXP]),
            deps=[bc1])
    bcm = A("v", lambda e: e.tensor_scalar(cntb[:], cntb[:], cm16[:, :],
                                           None, ALU.mult),
            deps=[bce, ecm])
    cb1 = A("y", lambda e: e.dma_start(out=D["cntin"].ap()[:, :],
                                       in_=cntb[:]),
            deps=[bcm], dma=True)
    cg = A("g", lambda e: e.collective_compute(
        "ReduceScatter", ALU.add, replica_groups=REPL,
        ins=[D["cntin"].ap().opt()], outs=[D["cnt2d"].ap().opt()]),
        deps=[cb1])
    cl = A("y", lambda e: e.dma_start(out=cnt2[:],
                                      in_=D["cnt2d"].ap()[:, :]),
        deps=[cg], dma=True)
    pm = A("t", lambda e: e.transpose(
        pP[0:EXP, 0:2], cnt2[:], consts["ident_f32"][0:2, 0:2]),
        deps=[cl, bc1])
    pe = A("v", lambda e: e.tensor_copy(prior[:], pP[0:EXP, 0:1]),
           deps=[pm])
    if lv == 0:
        gld = A("y", lambda e: e.dma_start(out=goffr[:], in_=d_goff0[:]),
                dma=True)
        goff_ready = gld
    else:
        ge = A("v", lambda e: e.tensor_copy(gcol[:], pP[0:EXP, 1:2]),
               deps=[pm])
        gt = A("t", lambda e: e.transpose(
            pL[0:1, 0:EXP], gcol[:], consts["ident_f32"][0:EXP, 0:EXP]),
            deps=[ge])
        gv = A("v", lambda e: e.tensor_copy(tr8[:], pL[0:1, 0:EXP]),
               deps=[gt])
        g0 = A("v", lambda e: e.tensor_scalar(tr8[:], tr8[:], 1023.0, None,
                                              ALU.add), deps=[gv])
        g1 = A("v", lambda e: e.tensor_scalar(
            tr8b[:], tr8[:], 1.0 / 1024.0, 1e-4, ALU.mult, ALU.add),
            deps=[g0])
        g2 = A("v", lambda e: e.tensor_copy(tru8[:], tr8b[:]), deps=[g1])
        g2b = A("v", lambda e: e.tensor_copy(tr8b[:], tru8[:]), deps=[g2])
        gr_ = A("v", lambda e: e.scalar_tensor_tensor(
            trr[:], tr8b[:], -1024.0, tr8[:], ALU.mult, ALU.add),
            deps=[g2b])
        ga1 = A("v", lambda e: e.tensor_scalar(tra[:], trr[:], 1024.0, None,
                                               ALU.is_ge), deps=[gr_])
        ga2 = A("v", lambda e: e.tensor_tensor(tr8b[:], tr8b[:], tra[:],
                                               ALU.add), deps=[ga1])
        ga3 = A("v", lambda e: e.tensor_scalar(tra[:], trr[:], 0.0, None,
                                               ALU.is_lt), deps=[ga2])
        ga4 = A("v", lambda e: e.tensor_tensor(tr8b[:], tr8b[:], tra[:],
                                               ALU.subtract), deps=[ga3])
        g3 = A("v", lambda e: e.tensor_scalar(tr8[:], tr8b[:], 1024.0, None,
                                              ALU.mult), deps=[ga4])
        g4 = A("v", lambda e: e.tensor_tensor_scan(
            tr8b[:], tr8[:], zrow[:], 0.0, ALU.add, ALU.add),
            deps=[g3, zrw])
        g5 = A("v", lambda e: e.tensor_tensor(pofr[:], tr8b[:], tr8[:],
                                              ALU.subtract), deps=[g4])
        g6 = A("v", lambda e: e.tensor_copy(goffr[:], pofr[:]), deps=[g5])
        goff_ready = g6
    gb = A("t", lambda e: e.matmul(
        pP[:, 8:8 + EXP], consts["ones_row_f32"][0:1, :], goffr[0:1, :],
        start=True, stop=True), deps=[goff_ready, pe])
    gbe = A("v", lambda e: e.tensor_copy(goff_bc[:], pP[:, 8:8 + EXP]),
            deps=[gb])
    gr1 = A("v", lambda e: e.tensor_tensor(crow[:], crow[:], mrow[:],
                                           ALU.subtract), deps=[sc1, tcr])
    gr2 = A("v", lambda e: e.tensor_scalar(crow[:], crow[:], prior[:], None,
                                           ALU.add), deps=[gr1, pe])

    prev_grT = [None, None]
    prev_sc = [None, None]
    prev_xsc = [None, None]
    prev_sct_any = None
    prev_xsct_any = None
    prev_floor = [None]
    sc_done = []
    for i in range(NT):
        b = i % 2
        pT = pA if b == 0 else pB
        tr_g = A("t", lambda e, i=i, pT=pT: e.transpose(
            pT[:, 0:8], crow[:, i * 128:(i + 1) * 128],
            consts["ident_f32"][0:8, 0:8]),
            deps=[gr2, prev_grT[b]])
        ev_g = A("v", lambda e, b=b, pT=pT: e.tensor_copy(
            grT[b][:], pT[:, 0:8]), deps=[tr_g])
        prev_grT[b] = ev_g
        ad_g = A("v", lambda e, b=b: e.tensor_tensor(
            grk2[b][:], grT[b][:], goff_bc[:], ALU.add),
            deps=[ev_g, gbe])
        col0 = A("v", lambda e, b=b, i=i: e.tensor_scalar(
            scat[b][:, 0:1], tokid0[:], float(i * 128), None, ALU.add),
            deps=[lc[1], prev_sc[b]])
        zc = A("v", lambda e, b=b: e.memset(scat[b][:, 2:8], 0.0),
               deps=[prev_sc[b]]) if i < 2 else None

        last = ad_g
        for j in range(topk):
            ec = tokinfo[:, i * 4 + j:i * 4 + j + 1]
            o1 = A("v", lambda e, j=j, ec=ec: e.tensor_scalar(
                oh[j][:], iot8[:], ec[:, :], None, ALU.is_equal),
                deps=[last, prev_oh[j]])
            prev_oh[j] = o1
            m1 = A("v", lambda e, b=b, j=j: e.tensor_tensor(
                t8[:], oh[j][:], grk2[b][:], ALU.mult), deps=[o1, ad_g])
            r1 = A("v", lambda e, j=j: e.reduce_sum(
                slc[:, j:j + 1], t8[:], axis=AX.X), deps=[m1])
            if lv == 0:
                m2 = A("v", lambda e, b=b, j=j: e.tensor_tensor(
                    t8[:], oh[j][:], grT[b][:], ALU.mult),
                    deps=[r1, ev_g])
                r2 = A("v", lambda e, j=j: e.reduce_sum(
                    rkc[:, j:j + 1], t8[:], axis=AX.X), deps=[m2])
                vl = A("v", lambda e, j=j: e.tensor_scalar(
                    vlc[:, j:j + 1], rkc[:, j:j + 1], float(CAP), None,
                    ALU.is_lt), deps=[r2])
                last = vl
            else:
                last = r1
            sj = slc[:, j:j + 1]

            def _floor(xap, m, ucol, deps_):
                f1 = A("v", lambda e, xap=xap, m=m, ucol=ucol:
                       e.tensor_scalar(ucol[:, :], xap[:, :], 1.0 / m, 1e-4,
                                       ALU.mult, ALU.add),
                       deps=deps_ + [prev_floor[0]])
                f2 = A("v", lambda e, ucol=ucol, j=j: e.tensor_copy(
                    tu_[:, j:j + 1], ucol[:, :]), deps=[f1])
                f3 = A("v", lambda e, ucol=ucol, j=j: e.tensor_copy(
                    ucol[:, :], tu_[:, j:j + 1]), deps=[f2])
                f4 = A("v", lambda e, xap=xap, m=m, ucol=ucol, j=j:
                       e.scalar_tensor_tensor(th_[:, j:j + 1], ucol[:, :],
                                              -float(m), xap[:, :],
                                              ALU.mult, ALU.add), deps=[f3])
                f5 = A("v", lambda e, m=m, j=j: e.tensor_scalar(
                    ti2_[:, j:j + 1], th_[:, j:j + 1], float(m), None,
                    ALU.is_ge), deps=[f4])
                f6 = A("v", lambda e, ucol=ucol, j=j: e.tensor_tensor(
                    ucol[:, :], ucol[:, :], ti2_[:, j:j + 1], ALU.add),
                    deps=[f5])
                f7 = A("v", lambda e, j=j: e.tensor_scalar(
                    ti2_[:, j:j + 1], th_[:, j:j + 1], 0.0, None,
                    ALU.is_lt), deps=[f6])
                f8 = A("v", lambda e, ucol=ucol, j=j: e.tensor_tensor(
                    ucol[:, :], ucol[:, :], ti2_[:, j:j + 1],
                    ALU.subtract), deps=[f7])
                prev_floor[0] = f8
                return f8

            secc = tb_[:, j:j + 1]
            qc = tc_[:, j:j + 1]
            fs = _floor(sj, CAP, secc, [last])
            w2 = A("v", lambda e, sj=sj, j=j: e.scalar_tensor_tensor(
                ta[:, j:j + 1], tb_[:, j:j + 1], -float(CAP), sj[:, :],
                ALU.mult, ALU.add), deps=[fs])
            gv = A("v", lambda e, sj=sj, j=j: e.scalar_tensor_tensor(
                tg_[:, j:j + 1], tb_[:, j:j + 1], 128.0, sj[:, :],
                ALU.mult, ALU.add), deps=[w2])
            fq = _floor(ta[:, j:j + 1], 128, qc, [gv])
            w5 = A("v", lambda e, j=j: e.scalar_tensor_tensor(
                ta[:, j:j + 1], tc_[:, j:j + 1], -128.0, ta[:, j:j + 1],
                ALU.mult, ALU.add), deps=[fq])
            w7 = A("v", lambda e, j=j: e.scalar_tensor_tensor(
                tc_[:, j:j + 1], ta[:, j:j + 1], float(NRC2),
                tc_[:, j:j + 1], ALU.mult, ALU.add), deps=[w5])
            w8 = A("v", lambda e, j=j: e.scalar_tensor_tensor(
                rowf[:, j:j + 1], tb_[:, j:j + 1], float(SECROWS),
                tc_[:, j:j + 1], ALU.mult, ALU.add), deps=[w7])
            if lv == 0:
                d1 = A("v", lambda e, ec=ec, j=j: e.scalar_tensor_tensor(
                    ta[:, j:j + 1], ec[:, :], float(SECROWS),
                    dropc[:, j:j + 1], ALU.mult, ALU.add),
                    deps=[w8, lc[2]])
                d2 = A("v", lambda e, j=j: e.tensor_tensor(
                    rowf[:, j:j + 1], rowf[:, j:j + 1], ta[:, j:j + 1],
                    ALU.subtract), deps=[d1])
                d3 = A("v", lambda e, j=j: e.tensor_scalar(
                    rowf[:, j:j + 1], rowf[:, j:j + 1], vlc[:, j:j + 1],
                    None, ALU.mult), deps=[d2])
                d4 = A("v", lambda e, j=j: e.tensor_tensor(
                    rowf[:, j:j + 1], rowf[:, j:j + 1], ta[:, j:j + 1],
                    ALU.add), deps=[d3])
                rowdone = d4
                z1 = A("v", lambda e, ec=ec, j=j: e.scalar_tensor_tensor(
                    ta[:, j:j + 1], ec[:, :], float(SECOUT), dropg[:, :],
                    ALU.mult, ALU.add), deps=[rowdone, edg])
                z2 = A("v", lambda e, j=j: e.tensor_tensor(
                    tg_[:, j:j + 1], tg_[:, j:j + 1], ta[:, j:j + 1],
                    ALU.subtract), deps=[z1])
                z3 = A("v", lambda e, j=j: e.tensor_scalar(
                    tg_[:, j:j + 1], tg_[:, j:j + 1], vlc[:, j:j + 1],
                    None, ALU.mult), deps=[z2])
                s3_ = A("v", lambda e, i=i, j=j: e.tensor_tensor(
                    slotinfo[:, i * 2 + j:i * 2 + j + 1], tg_[:, j:j + 1],
                    ta[:, j:j + 1], ALU.add), deps=[z3])
            else:
                rowdone = w8
                s3_ = A("v", lambda e, i=i, j=j: e.tensor_copy(
                    slotinfo[:, i * 2 + j:i * 2 + j + 1], tg_[:, j:j + 1]),
                    deps=[rowdone])
            wc = A("v", lambda e, b=b, i=i, j=j: e.tensor_copy(
                scat[b][:, 1:2], tokinfo[:, i * 4 + 2 + j:i * 4 + 3 + j]),
                deps=[rowdone, prev_sc[b]])
            cx = A("v", lambda e, b=b, j=j: e.tensor_copy(
                scix[b][:, j:j + 1], rowf[:, j:j + 1]),
                deps=[wc])
            if not (MOE_NOSC & 1):
                sct = A("g", lambda e, b=b, j=j: e.indirect_dma_start(
                    out=D["rec"][:, :],
                    out_offset=IndirectOffsetOnAxis(scix[b][:, j:j + 1], 0),
                    in_=scat[b][:, :], in_offset=None),
                    deps=[cx, col0, zc, zs, prev_sct_any], dma=True)
                sc_done.append(sct)
                prev_sct_any = sct
                prev_sc[b] = sct
            last = s3_
    for i in range(NT):
        b = i % 2
        ldx = None
        if not (MOE_NOSC & 4):
            ldx = A("y", lambda e, b=b, i=i: e.dma_start(
                out=xbf[b][:],
                in_=D["lnxin"].ap()[i * 128:(i + 1) * 128, :]),
                deps=[prev_xsc[b]], dma=True)
        for j in range(topk):
            xc = A("v", lambda e, b=b, j=j, i=i: e.tensor_copy(
                xsix[b][:, j:j + 1], slotinfo[:, i * 2 + j:i * 2 + j + 1]),
                deps=[sc_done[-1] if i == 0 and j == 0 else None,
                      prev_xsc[b]])
            if not (MOE_NOSC & 2):
                xsc = A("g", lambda e, b=b, j=j: e.indirect_dma_start(
                    out=D["slotin"][:, :],
                    out_offset=IndirectOffsetOnAxis(xsix[b][:, j:j + 1], 0),
                    in_=xbf[b][:, :], in_offset=None),
                    deps=[xc, ldx, prev_xsct_any] +
                    (sc_done[-1:] if i == 0 else []) +
                    ([zxs[-1]] if i < 2 and zxs else []),
                    dma=True)
                sc_done.append(xsc)
                prev_xsct_any = xsc
                prev_xsc[b] = xsc
    rsd = A("g", lambda e: e.collective_compute(
        "ReduceScatter", ALU.add, replica_groups=REPL,
        ins=[D["rec"].ap().opt()], outs=[D["recs"].ap().opt()]),
        deps=sc_done)
    rsx = A("g", lambda e: e.collective_compute(
        "ReduceScatter", ALU.add, replica_groups=REPL,
        ins=[D["slotin"].ap().opt()], outs=[D["slotmy"].ap().opt()]),
        deps=sc_done + zxs)
    rl = A("y", lambda e: e.dma_start(
        out=recsb[:],
        in_=D["recs"].ap().rearrange("(a b) w -> a (b w)", a=128)[:, :]),
        deps=[rsd], dma=True)
    A("v", lambda e: e.nop(), deps=[rl, rsx])
    st.emit(nc)
    ctx.barrier()
    return


def _moe_expert(ctx, consts, lv, P, D, recsb, pofr, tag):
    nc = ctx.nc
    st = Steps(ctx)
    A = st.add
    NBG = 1

    W1sb = [ctx.sbf(tag + f"w1_{c}", [128, MOE_I], BF16) for c in range(NDC)]
    W2sb = [ctx.sbf(tag + f"w2_{m}", [128, DM], BF16)
            for m in range(MOE_I // 128)]
    xg = [[ctx.sbf(tag + f"xg{b}_{j}", [128, DM], BF16) for j in range(4)]
          for b in range(NBG)]
    gyw = ctx.sbf(tag + "gyw", [128, 4], F32)
    gyr = ctx.sbf(tag + "gyr", [128, 4], F32)
    gyu = ctx.sbf(tag + "gyu", [128, 4], U32)
    gh_ = ctx.sbf(tag + "gh", [128, 1], F32)
    gi_ = ctx.sbf(tag + "gi", [128, 1], F32)
    gu_ = ctx.sbf(tag + "gu", [128, 1], U32)
    zgy = ctx.sbf(tag + "zgy", [128, 512], BF16)
    gdump = ctx.sbf(tag + "gdmp", [128, 1], F32)
    gdumpr = ctx.sbf(tag + "gdmr", [1, 128], F32)
    d_gdump = ctx.const("moe_gdump",
                        (float(T) + np.arange(128, dtype=np.float32))
                        .reshape(1, 128))
    sf = [[ctx.sbf(tag + f"sf{b}_{c}", [128, 512], BF16) for c in range(NDC)]
          for b in range(NBG)]
    hid = [ctx.sbf(tag + f"hd{m}", [128, 512], BF16)
           for m in range(MOE_I // 128)]
    ob = [ctx.sbf(tag + "ob0", [128, 512], BF16)] * 2
    osl = [ctx.sbf(tag + f"os{j}", [128, DM], BF16) for j in range(4)]
    if lv == 1:
        b1r = ctx.sbf(tag + "b1r", [32, 128], F32)
        b2r = b1r
        b1c = ctx.sbf(tag + "b1c", [128, MOE_I // 128], F32)
        b2c = ctx.sbf(tag + "b2c", [128, NDC], F32)
        tb1 = ctx.sbf(tag + "tb1", [32, 1], F32)
        egr = ctx.sbf(tag + "egr", [1, EXP], F32)
        egs = ctx.sbf(tag + "egs", [1, 1], F32)
        egb = ctx.sbf(tag + "egb", [128, 1], F32)
        base1 = ctx.sbf(tag + "bs1", [128, 1], F32)
        base2 = ctx.sbf(tag + "bs2", [128, 1], F32)
        tmpc = ctx.sbf(tag + "tmc", [128, 1], F32)
        idxw1 = ctx.sbf(tag + "ix1", [128, NDC], U32)
        idxw2 = ctx.sbf(tag + "ix2", [128, MOE_I // 128], U32)
        idxb1 = ctx.sbf(tag + "ixb1", [32, 1], U32)
        idxb2 = ctx.sbf(tag + "ixb2", [8, 1], U32)
        mb_sb = ctx.sbf(tag + "mb", [1, 2], F32)
        iotc = ctx.sbf(tag + "ioc", [128, 1], F32)
        d_iotc = ctx.const("moe_iotc",
                           np.arange(128, dtype=np.float32).reshape(128, 1))

    pW1 = [ctx.pf[0], ctx.pf[1]]
    pW2 = [ctx.pf[2], ctx.pf[3]]
    pE = ctx.pf[4]
    pBc = ctx.pf[5]
    pTb = [ctx.pb[0], ctx.pb[1]]

    wready = []
    if lv == 0:
        for c in range(NDC):
            wready.append(A("y", lambda e, c=c: e.dma_start(
                out=W1sb[c][:],
                in_=D["w1bf"].ap()[c * 128:(c + 1) * 128, :]), dma=True))
        for m in range(MOE_I // 128):
            wready.append(A("y", lambda e, m=m: e.dma_start(
                out=W2sb[m][:],
                in_=D["w2bf"].ap()[m * 128:(m + 1) * 128, :]), dma=True))
        w1_ready = [wready[c] for c in range(NDC)]
        w2_ready = [wready[NDC + m] for m in range(MOE_I // 128)]
        b1c_ready = b2c_ready = None
    else:
        lmb = A("y", lambda e: e.dma_start(out=mb_sb[:],
                                           in_=P["mebase"].ap()[:, :]),
                dma=True)
        lio = A("y", lambda e: e.dma_start(out=iotc[:], in_=d_iotc[:]),
                dma=True)

    zm1 = A("v", lambda e: e.memset(zgy[:], 0.0))
    ldur = A("y", lambda e: e.dma_start(out=gdumpr[:], in_=d_gdump[:]),
             dma=True)
    tdu = A("t", lambda e: e.transpose(
        pE[:, 1:2], gdumpr[:], consts["ident_f32"][0:1, 0:1]),
        deps=[ldur])
    ldu = A("v", lambda e: e.tensor_copy(gdump[:], pE[:, 1:2]),
            deps=[tdu])
    gyzs = []
    gyview = D["gy"].ap().rearrange("(a b) d -> a (b d)", a=128)
    per = gyview.shape[1]
    off = 0
    while off < per:
        w = min(512, per - off)
        gyzs.append(A("y", lambda e, off=off, w=w: e.dma_start(
            out=gyview[:, off:off + w], in_=zgy[:, 0:w]),
            deps=[zm1, gyzs[-1] if gyzs else None], dma=True))
        off += w

    last_intr = [None] * NBG
    prev_gych = [None]
    prev_gysct = [None]
    last_w1mm_all = None
    last_w2mm_all = None
    last_hid_evac = [None] * (MOE_I // 128)
    last_sfev_any = None
    last_store = [None] * 4
    last_oslev = {}
    last_obtr = {}
    last_obev = {}
    last_gather = [[None] * 4 for _ in range(NBG)]
    w1g_prev = w2g_prev = None
    b1g_prev = b2g_prev = None
    w1g_steps = w2g_steps = None
    b1c_last = b2c_last = None
    all_stores = []

    for g in range(NSG if MOE_EG is None else MOE_EG):
        gb2 = g % NBG
        bb = g // 2
        if lv == 1 and gb2 == 0:
            prev_w1mm = last_w1mm_all
            e1 = A("v", lambda e, bb=bb: e.tensor_scalar(
                egr[:], pofr[:], float(bb * 1024), None, ALU.subtract),
                deps=[lmb])
            e2 = A("v", lambda e: e.tensor_scalar(
                egr[:], egr[:], mb_sb[0:1, 0:1], None, ALU.is_le),
                deps=[e1])
            e3 = A("v", lambda e: e.reduce_sum(egs[:], egr[:],
                                               axis=AX.X), deps=[e2])
            e4 = A("v", lambda e: e.tensor_scalar(
                egs[:], egs[:], 1.0, None, ALU.subtract),
                deps=[e3])
            e5 = A("t", lambda e: e.matmul(
                pE[:, 0:1], consts["ones_row_f32"][0:1, :], egs[:],
                start=True, stop=True), deps=[e4, ldu])
            e6 = A("v", lambda e: e.tensor_copy(egb[:], pE[:, 0:1]),
                   deps=[e5])
            bs1 = A("v", lambda e: e.scalar_tensor_tensor(
                base1[:], egb[:], float(DM), iotc[:], ALU.mult, ALU.add),
                deps=[e6, lio])
            bs2 = A("v", lambda e: e.scalar_tensor_tensor(
                base2[:], egb[:], float(MOE_I), iotc[:], ALU.mult, ALU.add),
                deps=[e6, lio])
            w1g_steps, w2g_steps = [], []
            prev_ix = None
            for c in range(NDC):
                i1 = A("v", lambda e, c=c: e.tensor_scalar(
                    tmpc[:], base1[:], float(c * 128), None, ALU.add),
                    deps=[bs1, prev_ix])
                i2 = A("v", lambda e, c=c: e.tensor_copy(
                    idxw1[:, c:c + 1], tmpc[:]),
                    deps=[i1, w1g_prev[c] if w1g_prev else None])
                prev_ix = i2
                w1g_steps.append(A("g", lambda e, c=c: e.indirect_dma_start(
                    out=W1sb[c][:], out_offset=None,
                    in_=D["w1bf"][:, :],
                    in_offset=IndirectOffsetOnAxis(idxw1[:, c:c + 1], 0)),
                    deps=[i2, prev_w1mm], dma=True))
            for m in range(MOE_I // 128):
                i1 = A("v", lambda e, m=m: e.tensor_scalar(
                    tmpc[:], base2[:], float(m * 128), None, ALU.add),
                    deps=[bs2, prev_ix])
                i2 = A("v", lambda e, m=m: e.tensor_copy(
                    idxw2[:, m:m + 1], tmpc[:]),
                    deps=[i1, w2g_prev[m] if w2g_prev else None])
                prev_ix = i2
                w2g_steps.append(A("g", lambda e, m=m: e.indirect_dma_start(
                    out=W2sb[m][:], out_offset=None,
                    in_=D["w2bf"][:, :],
                    in_offset=IndirectOffsetOnAxis(idxw2[:, m:m + 1], 0)),
                    deps=[i2, last_w2mm_all], dma=True))
            nhid = MOE_I // 128
            ib1a = A("v", lambda e: e.scalar_tensor_tensor(
                tb1[:], egb[0:32, 0:1], float(nhid), iotc[0:32, 0:1],
                ALU.mult, ALU.add), deps=[e6, lio, b1g_prev])
            ib1b = A("v", lambda e: e.tensor_copy(idxb1[:], tb1[:]),
                     deps=[ib1a, b1g_prev])
            gb1 = A("g", lambda e: e.indirect_dma_start(
                out=b1r[:], out_offset=None,
                in_=P["b1"].ap().rearrange("e (m w) -> (e m) w", w=128),
                in_offset=IndirectOffsetOnAxis(idxb1[:], 0)),
                deps=[ib1b, b1c_last], dma=True)
            ib2a = A("v", lambda e: e.scalar_tensor_tensor(
                tb1[0:8, 0:1], egb[0:8, 0:1], float(NDC), iotc[0:8, 0:1],
                ALU.mult, ALU.add), deps=[ib1b, b2g_prev])
            ib2b = A("v", lambda e: e.tensor_copy(idxb2[:], tb1[0:8, 0:1]),
                     deps=[ib2a, b2g_prev])
            tr1 = A("t", lambda e: e.transpose(
                pBc[:, 0:32], b1r[:], consts["ident_f32"][0:32, 0:32]),
                deps=[gb1, b1c_last])
            gb2_ = A("g", lambda e: e.indirect_dma_start(
                out=b2r[0:8, :], out_offset=None,
                in_=P["b2"].ap().rearrange("e (m w) -> (e m) w", w=128),
                in_offset=IndirectOffsetOnAxis(idxb2[:], 0)),
                deps=[ib2b, b2c_last, tr1], dma=True)
            ev1 = A("v", lambda e: e.tensor_copy(b1c[:], pBc[:, 0:32]),
                    deps=[tr1] + [d for d in last_hid_evac if d])
            tr2 = A("t", lambda e: e.transpose(
                pBc[:, 32:40], b2r[0:8, :], consts["ident_f32"][0:8, 0:8]),
                deps=[gb2_, b2c_last, tr1])
            ev2 = A("v", lambda e: e.tensor_copy(b2c[:], pBc[:, 32:40]),
                    deps=[tr2] + [last_obev[k] for k in last_obev])
            b1c_last, b2c_last = ev1, ev2
            w1g_prev, w2g_prev = w1g_steps, w2g_steps
            b1g_prev, b2g_prev = gb1, gb2_
            w1_ready = w1g_steps
            w2_ready = w2g_steps
            b1c_ready, b2c_ready = b1c_last, b2c_last

        gsteps = []
        for j in range(4):
            k = g * 4 + j
            gsteps.append(A("y", lambda e, gb2=gb2, j=j, k=k: e.dma_start(
                out=xg[gb2][j][:],
                in_=D["slotmy"].ap()[k * 128:(k + 1) * 128, :]),
                deps=[last_intr[gb2]], dma=True))
            last_gather[gb2][j] = gsteps[-1]

        if MOE_GLVL < 2:
            continue
        sfev = []
        intr = []
        kk = 0
        for c in range(NDC):
            for j in range(4):
                pk = pTb[kk % 2]
                t1 = A("t", lambda e, gb2=gb2, c=c, j=j, pk=pk: e.transpose(
                    pk[:, 0:128], xg[gb2][j][:, c * 128:(c + 1) * 128],
                    consts["ident_bf"][:]),
                    deps=[gsteps[j], sfev[kk - 2] if kk >= 2
                          else last_sfev_any,
                          None if kk >= 2 else last_oslev.get("any")])
                t2 = A("v", lambda e, gb2=gb2, c=c, j=j, pk=pk:
                       e.tensor_copy(sf[gb2][c][:, j * 128:(j + 1) * 128],
                                     pk[:, 0:128]),
                       deps=[t1, last_w1mm_all])
                intr.append(t1)
                sfev.append(t2)
                kk += 1
        last_intr[gb2] = intr[-1]
        last_sfev_any = sfev[-1]

        if MOE_GLVL < 3:
            continue
        if MOE_DBG3 and g == 0:
            A("y", lambda e, gb2=gb2: e.dma_start(
                out=MOE_DBG3_T["sf"].ap()[:, :], in_=sf[gb2][0][:]),
                deps=sfev[0:4], dma=True)
            for j in range(4):
                A("y", lambda e, gb2=gb2, j=j: e.dma_start(
                    out=MOE_DBG3_T["xg"].ap()[j * 128:(j + 1) * 128, :],
                    in_=xg[gb2][j][:]), deps=[intr[-1]], dma=True)
            A("y", lambda e: e.dma_start(
                out=MOE_DBG3_T["w1"].ap()[:, :],
                in_=W1sb[0][:, 0:512]), deps=list(w1_ready), dma=True)
            A("y", lambda e: e.dma_start(
                out=MOE_DBG3_T["w2"].ap()[:, :],
                in_=W2sb[0][:, 0:512]), deps=list(w2_ready), dma=True)
        w1mm = []
        nhid = MOE_I // 128
        for m in range(nhid):
            pk = pW1[m % 2]
            deps = sfev + list(w1_ready) + [last_w2mm_all]
            deps.append(last_hid_evac[m - 2] if m >= 2 else None)
            if lv == 1:
                deps.append(b1c_ready)
            mmst = A("t", lambda e, m=m, gb2=gb2, pk=pk: [
                e.matmul(pk[:, 0:512], W1sb[c][:, m * 128:(m + 1) * 128],
                         sf[gb2][c][:], start=(c == 0), stop=(c == NDC - 1))
                for c in range(NDC)][-1], deps=deps)
            w1mm.append(mmst)
            if lv == 1:
                ev = A("s", lambda e, m=m, pk=pk: e.activation(
                    hid[m][:], pk[:, 0:512], AF.Relu,
                    bias=b1c[:, m:m + 1]), deps=[mmst])
            else:
                ev = A("s", lambda e, m=m, pk=pk: e.activation(
                    hid[m][:], pk[:, 0:512], AF.Relu), deps=[mmst])
            last_hid_evac[m] = ev
        last_w1mm_all = w1mm[-1]
        if MOE_DBG3 and g == 0:
            A("y", lambda e: e.dma_start(
                out=MOE_DBG3_T["hid"].ap()[:, :], in_=hid[0][:]),
                deps=[last_hid_evac[0]], dma=True)

        if MOE_GLVL < 4:
            continue
        oslev = []
        w2mm_steps = []
        for d2 in range(NDC):
            pk = pW2[d2 % 2]
            deps = list(last_hid_evac) + list(w2_ready)
            deps.append(last_obev.get(("bank", d2 % 2)))
            if lv == 1:
                deps.append(b2c_ready)
            mmst = A("t", lambda e, d2=d2, pk=pk: [
                e.matmul(pk[:, 0:512], W2sb[m][:, d2 * 128:(d2 + 1) * 128],
                         hid[m][:], start=(m == 0), stop=(m == nhid - 1))
                for m in range(nhid)][-1], deps=deps)
            w2mm_steps.append(mmst)
            if lv == 1:
                obv = A("v", lambda e, d2=d2, pk=pk: e.tensor_scalar(
                    ob[d2 % 2][:], pk[:, 0:512], b2c[:, d2:d2 + 1], None,
                    ALU.add),
                    deps=[mmst, last_obtr.get(d2 % 2), b2c_ready])
            else:
                obv = A("v", lambda e, d2=d2, pk=pk: e.tensor_copy(
                    ob[d2 % 2][:], pk[:, 0:512]),
                    deps=[mmst, last_obtr.get(d2 % 2)])
            last_obev[("bank", d2 % 2)] = obv
            last_obev[d2] = obv
            if MOE_DBG3 and g == 0 and d2 == 0:
                A("y", lambda e: e.dma_start(
                    out=MOE_DBG3_T["ob"].ap()[:, :], in_=ob[0][:]),
                    deps=[obv], dma=True)
            if MOE_GLVL < 5:
                continue
            for j in range(4):
                ko = d2 * 4 + j
                pk = pTb[ko % 2]
                tro = A("t", lambda e, d2=d2, j=j, pk=pk: e.transpose(
                    pk[:, 0:128], ob[d2 % 2][:, j * 128:(j + 1) * 128],
                    consts["ident_bf"][:]),
                    deps=[obv, oslev[ko - 2] if ko >= 2
                          else last_oslev.get("any")])
                last_obtr[d2 % 2] = tro
                oev = A("v", lambda e, d2=d2, j=j, pk=pk: e.tensor_copy(
                    osl[j][:, d2 * 128:(d2 + 1) * 128], pk[:, 0:128]),
                    deps=[tro] + ([last_store[jj] for jj in range(4)]
                                  if ko < 2 else []))
                oslev.append(oev)
        last_w2mm_all = w2mm_steps[-1]
        if MOE_GLVL < 5:
            continue
        last_oslev["any"] = oslev[-1]
        dbgst = None
        for j in range(4):
            k = g * 4 + j
            oevj = [oslev[d2 * 4 + j] for d2 in range(NDC)]
            wcp = A("v", lambda e, j=j, k=k: e.tensor_copy(
                gyw[:, j:j + 1], recsb[:, k * 8 + 1:k * 8 + 2]),
                deps=[last_store[j], prev_gych[0]])
            tok = recsb[:, k * 8:k * 8 + 1]
            f1 = A("v", lambda e, tok=tok, j=j: e.tensor_scalar(
                gh_[:], tok[:, :], 1.0 / T, 1e-4, ALU.mult, ALU.add),
                deps=[wcp])
            f2 = A("v", lambda e: e.tensor_copy(gu_[:], gh_[:]),
                   deps=[f1])
            f3 = A("v", lambda e: e.tensor_copy(gh_[:], gu_[:]),
                   deps=[f2])
            f4 = A("v", lambda e, tok=tok: e.scalar_tensor_tensor(
                gi_[:], gh_[:], -float(T), tok[:, :], ALU.mult, ALU.add),
                deps=[f3])
            f5 = A("v", lambda e, j=j: e.tensor_scalar(
                gyr[:, j:j + 1], gi_[:], float(T), None, ALU.is_ge),
                deps=[f4])
            f6 = A("v", lambda e, j=j: e.tensor_tensor(
                gh_[:], gh_[:], gyr[:, j:j + 1], ALU.add), deps=[f5])
            f7 = A("v", lambda e, j=j: e.tensor_scalar(
                gyr[:, j:j + 1], gi_[:], 0.0, None, ALU.is_lt),
                deps=[f6])
            f8 = A("v", lambda e, j=j: e.tensor_tensor(
                gh_[:], gh_[:], gyr[:, j:j + 1], ALU.subtract), deps=[f7])
            f9 = A("v", lambda e, tok=tok, j=j: e.scalar_tensor_tensor(
                gyr[:, j:j + 1], gh_[:], 128.0, tok[:, :], ALU.mult,
                ALU.add), deps=[f8])
            z0 = A("v", lambda e, j=j: e.tensor_scalar(
                gi_[:], gyw[:, j:j + 1], 0.0, None, ALU.is_equal),
                deps=[f9])
            z1 = A("v", lambda e, j=j: e.tensor_tensor(
                gh_[:], gdump[:], gyr[:, j:j + 1], ALU.subtract),
                deps=[z0, ldu])
            z2 = A("v", lambda e: e.tensor_scalar(
                gh_[:], gh_[:], gi_[:], None, ALU.mult), deps=[z1])
            z3 = A("v", lambda e, j=j: e.tensor_tensor(
                gyr[:, j:j + 1], gyr[:, j:j + 1], gh_[:], ALU.add),
                deps=[z2])
            cz = A("v", lambda e, j=j: e.tensor_copy(
                gyu[:, j:j + 1], gyr[:, j:j + 1]), deps=[z3])
            wm = A("v", lambda e, j=j: e.tensor_scalar(
                osl[j][:], osl[j][:], gyw[:, j:j + 1], None, ALU.mult),
                deps=[cz] + oevj + ([dbgst] if j == 0 else []))
            prev_gych[0] = wm
            if MOE_DBG3:
                A("y", lambda e, g=g, j=j: e.dma_start(
                    out=MOE_DBG3_T["rows"].ap()
                    [(g * 4 + j) * 128:(g * 4 + j + 1) * 128, :],
                    in_=gyr[:, j:j + 1]), deps=[cz], dma=True)
                A("y", lambda e, g=g, j=j: e.dma_start(
                    out=MOE_DBG3_T["wts"].ap()
                    [(g * 4 + j) * 128:(g * 4 + j + 1) * 128, :],
                    in_=gyw[:, j:j + 1]), deps=[wm], dma=True)
                if j == 0:
                    dbgst = A("y", lambda e, g=g: e.dma_start(
                        out=MOE_DBG3_T["osl"].ap()[g * 128:(g + 1) * 128, :],
                        in_=osl[0][:]),
                        deps=[oslev[d2 * 4] for d2 in range(NDC)] + [cz],
                        dma=True)
            stp = A("g", lambda e, j=j: e.indirect_dma_start(
                out=D["gy"][:, :],
                out_offset=IndirectOffsetOnAxis(gyu[:, j:j + 1], 0),
                in_=osl[j][:, :], in_offset=None),
                deps=[wm, prev_gysct[0], gyzs[-1]], dma=True)
            prev_gysct[0] = stp
            last_store[j] = stp
            all_stores.append(stp)

    rsg = A("g", lambda e: e.collective_compute(
        "ReduceScatter", ALU.add, replica_groups=REPL,
        ins=[D["gy"].ap().opt()], outs=[D["gys"].ap().opt()]),
        deps=all_stores + gyzs)
    A("v", lambda e: e.nop(), deps=[rsg])
    st.emit(nc)
    ctx.barrier()
    return


def _moe_gatherback(ctx, consts, lv, P, D, tokinfo, slotinfo, xin, xout,
                    tag):
    nc = ctx.nc
    st = Steps(ctx)
    A = st.add
    NB = 3

    xt = [ctx.sbf(tag + f"x{b}", [128, DM], F32) for b in range(NB)]
    gt = [ctx.sbf(tag + f"g{b}", [128, DM], BF16) for b in range(NB)]
    tmp = [ctx.sbf(tag + f"t{b}", [128, DM], F32) for b in range(NB)]

    prev_store = [None] * NB
    prev_axs = [None] * NB
    stores = []
    for i in range(NT):
        b = i % NB
        ld = A("y", lambda e, i=i, b=b: e.dma_start(
            out=xt[b][:], in_=xin[i * 128:(i + 1) * 128, :]),
            deps=[prev_store[b]], dma=True)
        lg = A("y", lambda e, i=i, b=b: e.dma_start(
            out=gt[b][:], in_=D["gys"].ap()[i * 128:(i + 1) * 128, :]),
            deps=[prev_axs[b]], dma=True)
        cv = A("v", lambda e, b=b: e.tensor_copy(tmp[b][:], gt[b][:]),
               deps=[lg, prev_axs[b]])
        axs = A("v", lambda e, b=b: e.tensor_tensor(
            xt[b][:], xt[b][:], tmp[b][:], ALU.add),
            deps=[ld, cv])
        prev_axs[b] = axs
        stp = A("y", lambda e, i=i, b=b: e.dma_start(
            out=xout[i * 128:(i + 1) * 128, :], in_=xt[b][:]),
            deps=[axs], dma=True)
        prev_store[b] = stp
        stores.append(stp)
    A("v", lambda e: e.nop(), deps=stores)
    st.emit(nc)
    ctx.barrier()
    return


def emit_moe_prologue(ctx, pairs):
    s = ctx.sem("MOE_pro")
    nc = ctx.nc
    n0 = s.v - s.b
    with nc.Block() as blk:
        @blk.gpsimd
        def _(g):
            n = 0
            for src, dst in pairs:
                s.inc(g.dma_start(out=dst.ap()[:, :], in_=src.ap()[:, :]), 16)
                n += 16
            g.wait_ge(s.h, s.tgt(n0 + n))
    ctx.barrier()


MOE_TRUNC = None
MOE_EG = None
MOE_GLVL = 5
MOE_GBL = 3
MOE_DBG = False
MOE_DBG3 = False
MOE_NOZX = False
MOE_DBG5 = False
MOE_NOSC = 0


def _moe_copyout(ctx, xin, xout, tag):
    s = ctx.sem("MOE_cp")
    n0 = s.v - s.b
    with ctx.nc.Block() as blk:
        @blk.sync
        def _(y):
            s.inc(y.dma_start(out=xout[:, :], in_=xin[:, :]), 16)
            y.wait_ge(s.h, s.tgt(n0 + 16))
    ctx.barrier()


def emit_moe(ctx, consts, lv, xin, xout, P, D, tag):
    with ctx.frame():
        tokinfo = ctx.sbf(tag + "ti", [128, NT * 4], F32)
        slotinfo = ctx.sbf(tag + "si", [128, NT * 2], F32)
        recsb = ctx.sbf(tag + "rb", [128, NRC2 * 8], F32)
        pofr = ctx.sbf(tag + "po", [1, EXP], F32)
        emit_ln(ctx, consts, xin, lnx32_hbm=D["lnx32"].ap(),
                lnxbf_hbm=D["lnxin"].ap(), tag=tag + "l")
        if MOE_TRUNC == "ln":
            _moe_copyout(ctx, xin, xout, tag)
            return
        with ctx.frame():
            _moe_routing(ctx, consts, lv, P, D, tokinfo, slotinfo, recsb,
                         pofr, tag + "r")
        if MOE_DBG:
            sdbg = ctx.sem("MOE_dbg")
            n0 = sdbg.v - sdbg.b
            nc = ctx.nc
            dti = nc.dram_tensor(tag + "d_ti", [128, NT * 4], F32,
                                 kind="ExternalOutput").ap()
            dsi = nc.dram_tensor(tag + "d_si", [128, NT * 2], F32,
                                 kind="ExternalOutput").ap()
            drb = nc.dram_tensor(tag + "d_rb", [128, NRC2 * 8], F32,
                                 kind="ExternalOutput").ap()
            with nc.Block() as blk:
                @blk.sync
                def _(y):
                    sdbg.inc(y.dma_start(out=dti[:, :], in_=tokinfo[:]), 16)
                    sdbg.inc(y.dma_start(out=dsi[:, :], in_=slotinfo[:]), 16)
                    sdbg.inc(y.dma_start(out=drb[:, :], in_=recsb[:]), 16)
                    y.wait_ge(sdbg.h, sdbg.tgt(n0 + 48))
            ctx.barrier()
        if MOE_TRUNC == "route":
            _moe_copyout(ctx, xin, xout, tag)
            return
        if MOE_TRUNC == "routeln":
            _moe_copyout(ctx, D["lnx32"].ap(), xout, tag)
            return
        with ctx.frame():
            _moe_expert(ctx, consts, lv, P, D, recsb, pofr, tag + "e")
        if MOE_DBG:
            sdb2 = ctx.sem("MOE_dbg2")
            n0 = sdb2.v - sdb2.b
            nc = ctx.nc
            dsm = nc.dram_tensor(tag + "d_sm", [SECOUT, DM], BF16,
                                 kind="ExternalOutput").ap()
            dgy = nc.dram_tensor(tag + "d_gy", [GYPAD, DM], BF16,
                                 kind="ExternalOutput").ap()
            with nc.Block() as blk:
                @blk.gpsimd
                def _(g):
                    sdb2.inc(g.dma_start(out=dsm[:, :],
                                         in_=D["slotmy"].ap()[:, :]), 16)
                    sdb2.inc(g.dma_start(out=dgy[:, :],
                                         in_=D["gys"].ap()[:, :]), 16)
                    g.wait_ge(sdb2.h, sdb2.tgt(n0 + 32))
            ctx.barrier()
        if MOE_TRUNC == "expert":
            _moe_copyout(ctx, xin, xout, tag)
            return
        with ctx.frame():
            _moe_gatherback(ctx, consts, lv, P, D, tokinfo, slotinfo,
                            xin, xout, tag + "g")
    return


def build_moe(lv):
    global MOE_DBG3_T, MOE_DBG5_T
    nc = bass.Bass("TRN2", target_bir_lowering=False, debug=False,
                   dynamic_dma_scratch_size=16384)
    ctx = Ctx(nc)
    if MOE_DBG5:
        MOE_DBG5_T = {
            "xt0": nc.dram_tensor("dbg5_xt0", [128, DM], F32,
                                  kind="ExternalOutput"),
            "lgT": nc.dram_tensor("dbg5_lgT", [8, 128], F32,
                                  kind="ExternalOutput"),
            "lg0": nc.dram_tensor("dbg5_lg0", [128, EXP], F32,
                                  kind="ExternalOutput"),
            "lg1": nc.dram_tensor("dbg5_lg1", [128, EXP], F32,
                                  kind="ExternalOutput"),
            "mxr": nc.dram_tensor("dbg5_mxr", [128, EXP], F32,
                                  kind="ExternalOutput"),
            "mxif": nc.dram_tensor("dbg5_mxif", [128, EXP], F32,
                                   kind="ExternalOutput")}
    if MOE_DBG3:
        MOE_DBG3_T = {
            "rows": nc.dram_tensor("dbg_rows", [40 * 128, 1], F32,
                                   kind="ExternalOutput"),
            "wts": nc.dram_tensor("dbg_wts", [40 * 128, 1], F32,
                                  kind="ExternalOutput"),
            "osl": nc.dram_tensor("dbg_osl", [NSG * 128, DM], BF16,
                                  kind="ExternalOutput"),
            "sf": nc.dram_tensor("dbg_sf", [128, 512], BF16,
                                 kind="ExternalOutput"),
            "hid": nc.dram_tensor("dbg_hid", [128, 512], BF16,
                                  kind="ExternalOutput"),
            "w1": nc.dram_tensor("dbg_w1", [128, 512], BF16,
                                 kind="ExternalOutput"),
            "w2": nc.dram_tensor("dbg_w2", [128, 512], BF16,
                                 kind="ExternalOutput"),
            "ob": nc.dram_tensor("dbg_ob", [128, 512], BF16,
                                 kind="ExternalOutput"),
            "xg": nc.dram_tensor("dbg_xg", [512, DM], BF16,
                                 kind="ExternalOutput")}
    xin = nc.dram_tensor("xin", [T, DM], F32, kind="ExternalInput").ap()
    xout = nc.dram_tensor("xout", [T, DM], F32, kind="ExternalOutput").ap()
    P = {"wrt": nc.dram_tensor("wrt", [DM, EXP], F32, kind="ExternalInput"),
         "brt": nc.dram_tensor("brt", [1, EXP], F32, kind="ExternalInput"),
         "cmask16": nc.dram_tensor("cmask16", [16, 1], F32,
                                   kind="ExternalInput"),
         "tokid0": nc.dram_tensor("tokid0", [128, 1], F32,
                                  kind="ExternalInput")}
    if lv == 0:
        P["w1"] = nc.dram_tensor("w1", [DM, MOE_I], F32,
                                 kind="ExternalInput")
        P["w2"] = nc.dram_tensor("w2", [MOE_I, DM], F32,
                                 kind="ExternalInput")
        w1bf_shape, w2bf_shape = [DM, MOE_I], [MOE_I, DM]
    else:
        P["w1"] = nc.dram_tensor("w1", [EXP * DM, MOE_I], F32,
                                 kind="ExternalInput")
        P["w2"] = nc.dram_tensor("w2", [EXP * MOE_I, DM], F32,
                                 kind="ExternalInput")
        P["b1"] = nc.dram_tensor("b1", [EXP, MOE_I], F32,
                                 kind="ExternalInput")
        P["b2"] = nc.dram_tensor("b2", [EXP, DM], F32, kind="ExternalInput")
        P["mebase"] = nc.dram_tensor("mebase", [1, 2], F32,
                                     kind="ExternalInput")
        w1bf_shape, w2bf_shape = [EXP * DM, MOE_I], [EXP * MOE_I, DM]
    D = {"lnx32": nc.dram_tensor("lnx32", [T, DM], F32, kind="Internal"),
         "lnxin": nc.dram_tensor("lnxin", [T, DM], BF16, kind="Internal"),
         "cntin": nc.dram_tensor("cntin", [16, EXP], F32, kind="Internal"),
         "cnt2d": nc.dram_tensor("cnt2d", [2, EXP], F32, kind="Internal"),
         "rec": nc.dram_tensor("rec", [RECROWS, 8], F32, kind="Internal"),
         "recs": nc.dram_tensor("recs", [SECROWS, 8], F32, kind="Internal"),
         "slotin": nc.dram_tensor("slotin", [GOUT, DM], BF16,
                                  kind="Internal"),
         "slotmy": nc.dram_tensor("slotmy", [SECOUT, DM], BF16,
                                  kind="Internal"),
         "gy": nc.dram_tensor("gy", [GYROWS, DM], BF16, kind="Internal"),
         "gys": nc.dram_tensor("gys", [GYPAD, DM], BF16, kind="Internal"),
         "w1bf": nc.dram_tensor("w1bf", w1bf_shape, BF16, kind="Internal"),
         "w2bf": nc.dram_tensor("w2bf", w2bf_shape, BF16, kind="Internal")}
    consts = make_consts(ctx)
    emit_moe_prologue(ctx, [(P["w1"], D["w1bf"]), (P["w2"], D["w2bf"])])
    emit_moe(ctx, consts, lv, xin, xout, P, D, "m")
    return nc


_PROGS = {}


def _prog(name):
    if name not in _PROGS:
        _PROGS[name] = {
            "l0a": build_l0_attn_a,
            "l0b": build_l0_attn_b,
            "l1a": build_l1_attn_a,
            "l1b": build_l1_attn_b,
            "l0f": lambda: build_moe(0),
            "l1f": lambda: build_moe(1),
        }[name]()
    return _PROGS[name]


def _rope_perm():
    return np.concatenate([np.arange(0, HD, 2), np.arange(1, HD, 2)])


def kernel(**inp):
    x = np.asarray(inp["L0_attn_a_x"], np.float32).reshape(TALL, DM)
    f32 = np.float32

    def cc(a):
        return np.ascontiguousarray(a, dtype=f32)

    w_l0a = {"wqkv": cc(inp["L0_attn_a_qkv_w"].T),
             "wproj": cc(inp["L0_attn_a_proj_w"].T),
             "alpha_log": cc(inp["L0_attn_a_alpha_log"]).reshape(1, NH),
             "beta_log": cc(inp["L0_attn_a_beta_log"]).reshape(1, NH)}
    w_l0b = {"wq": cc(inp["L0_attn_b_Wq"]), "wk": cc(inp["L0_attn_b_Wk"]),
             "wv": cc(inp["L0_attn_b_Wv"]), "wo": cc(inp["L0_attn_b_Wo"]),
             "wg": cc(inp["L0_attn_b_Wg"]),
             "wgk": cc(inp["L0_attn_b_Wgk"]),
             "wb": cc(inp["L0_attn_b_Wb"]),
             "bb": cc(inp["L0_attn_b_bb"]).reshape(1, NH),
             "alog": cc(inp["L0_attn_b_A_log"]).reshape(1, NH),
             "cqw": cc(inp["L0_attn_b_conv_q_w"].T),
             "ckw": cc(inp["L0_attn_b_conv_k_w"].T),
             "cvw": cc(inp["L0_attn_b_conv_v_w"].T),
             "gnw": cc(inp["L0_attn_b_gnorm_w"]).reshape(1, HD),
             "dtb": cc(inp["L0_attn_b_dt_bias"]).reshape(1, NH)}
    perm = _rope_perm()

    def permcols(w):
        w4 = np.asarray(w, f32).reshape(-1, NH, HD)
        return cc(w4[:, :, perm].reshape(w.shape))

    w_l1a = {"wq": permcols(inp["L1_attn_a_wq"]),
             "w_down": cc(inp["L1_attn_a_w_down"]),
             "wk_up": permcols(inp["L1_attn_a_wk_up"]),
             "wv_up": cc(inp["L1_attn_a_wv_up"]),
             "wo": cc(inp["L1_attn_a_wo"])}
    wq4 = np.asarray(inp["L1_attn_b_w_qkv"], f32).reshape(DM, NH, 3, HD)
    w_qkv_r = cc(np.concatenate(
        [wq4[:, :, j, :].reshape(DM, DM) for j in range(3)], axis=1))
    w_l1b = {"w_qkv": w_qkv_r, "w_out": cc(inp["L1_attn_b_w_out"]),
             "b_out": cc(inp["L1_attn_b_b_out"]).reshape(1, DM),
             "gamma": cc(inp["L1_attn_b_gamma"]).reshape(1, DM),
             "beta": cc(inp["L1_attn_b_beta"]).reshape(1, DM)}
    Wi = np.asarray(inp["L0_ffn_experts_inter"], f32)
    Wo_ = np.asarray(inp["L0_ffn_experts_out"], f32)
    w_l0f = {"wrt": cc(inp["L0_ffn_W_router"].T),
             "brt": cc(inp["L0_ffn_b_router"]).reshape(1, EXP)}
    W1f = np.asarray(inp["L1_ffn_W1"], f32)
    W2f = np.asarray(inp["L1_ffn_W2"], f32)
    w_l1f = {"wrt": cc(inp["L1_ffn_w_gate"]),
             "brt": cc(inp["L1_ffn_b_gate"]).reshape(1, EXP),
             "w1": cc(W1f.reshape(EXP * DM, MOE_I)),
             "w2": cc(W2f.reshape(EXP * MOE_I, DM)),
             "b1": cc(inp["L1_ffn_b1"]),
             "b2": cc(inp["L1_ffn_b2"])}

    def percore(w, extra=None):
        maps = []
        for c in range(NCORES):
            m = dict(w)
            if extra:
                m.update(extra(c))
            maps.append(m)
        return maps

    def moe_extra(lv):
        def f(c):
            m = {"cmask16": np.stack(
                [(c < np.arange(EXP)).astype(f32),
                 np.ones(EXP, f32)], axis=1).reshape(16, 1),
                "tokid0": (c * T + np.arange(128, dtype=f32))
                .reshape(128, 1)}
            if lv == 0:
                m["w1"] = cc(Wi[c])
                m["w2"] = cc(Wo_[c])
            else:
                m["mebase"] = np.array([[c * CAP, c]], f32)
            return m
        return f

    seq = [("l0a", w_l0a, None), ("l0f", w_l0f, moe_extra(0)),
           ("l0b", w_l0b, None), ("l0f", w_l0f, moe_extra(0)),
           ("l1a", w_l1a, None), ("l1f", w_l1f, moe_extra(1)),
           ("l1b", w_l1b, None), ("l1f", w_l1f, moe_extra(1))]
    for name, w, extra in seq:
        nc = _prog(name)
        maps = percore(w, extra)
        for c in range(NCORES):
            maps[c]["xin"] = cc(x[c * T:(c + 1) * T])
        res = run_bass_kernel_spmd(nc, maps, core_ids=list(range(NCORES)))
        x = np.concatenate([np.asarray(res.results[c]["xout"], f32)
                            for c in range(NCORES)], axis=0)
    return x.reshape(B, S, DM)
